# version 01 (final)
# baseline (speedup 1.0000x reference)
import numpy as np
from contextlib import ExitStack

import concourse.bass as bass
import concourse.mybir as mybir
from concourse.bass_utils import run_bass_kernel_spmd
from concourse.bass_core_rust import IndirectOffsetOnAxis

F32 = mybir.dt.float32
BF16 = mybir.dt.bfloat16
U32 = mybir.dt.uint32
AF = mybir.ActivationFunctionType
ALU = mybir.AluOpType
AX = mybir.AxisListType

NCORES = 8
D = 1024
DI = 2048
T = 2048
NST = 16
LN_EPS = 1e-5

ENGINES = ("sync", "gpsimd", "scalar", "vector", "tensor")


class Step:
    __slots__ = ("eng", "fn", "after", "n_incs", "sem", "count", "name",
                 "fence", "plug")

    def __init__(self, eng, fn, after, n_incs, name, fence=False, plug=False):
        self.eng, self.fn, self.after, self.n_incs = eng, fn, list(after), n_incs
        self.sem = None
        self.count = 0
        self.name = name
        self.fence = fence
        self.plug = plug


class Prog:

    def __init__(self, nc, ctx):
        self.nc = nc
        self.ctx = ctx
        self.steps = {e: [] for e in ENGINES}
        self.sems = {e: ctx.enter_context(nc.semaphore(f"ps_{e}")) for e in ENGINES}
        self.plugs = {e: ctx.enter_context(nc.sbuf_tensor(f"plug_{e}", [128, 2], F32))
                      for e in ("vector", "scalar", "gpsimd")}

    def step(self, eng, n_incs, fn, after=(), name="", fence=False, plug=False):
        s = Step(eng, fn, after, n_incs + (1 if plug else 0), name, fence, plug)
        self.steps[eng].append(s)
        return s

    def emit(self):
        for e in ENGINES:
            tot = 0
            for s in self.steps[e]:
                tot += s.n_incs
                s.sem = self.sems[e]
                s.count = tot
        block = self.ctx.enter_context(self.nc.Block())
        for eng_name, starter in (("sync", block.sync), ("gpsimd", block.gpsimd),
                                  ("scalar", block.scalar), ("vector", block.vector),
                                  ("tensor", block.tensor)):
            starter(self._body(eng_name))

    def _body(self, eng_name):
        steps = self.steps[eng_name]
        sem = self.sems[eng_name]

        prog = self

        def body(e):
            for s in steps:
                for dep in s.after:
                    e.wait_ge(dep.sem, dep.count)
                got = [0]
                base = s.count - s.n_incs

                def fin(ins, n=1, s=s, base=base, got=got):
                    ins.then_inc(sem, n)
                    got[0] += n
                    if s.fence:
                        e.wait_ge(sem, base + got[0])
                    return ins

                s.fn(e, fin)
                if s.plug:
                    if eng_name == "scalar":
                        fin(e.memzero(prog.plugs[eng_name][:]), 1)
                    else:
                        fin(e.memset(prog.plugs[eng_name][:], 0.0), 1)
                assert got[0] == s.n_incs, (
                    f"step {s.name} on {eng_name}: declared {s.n_incs} "
                    f"incs, emitted {got[0]}")
        return body


class _RowView:

    def __init__(self, t, w=None):
        self.t = t
        self.w = w if w is not None else t.shape[1] if hasattr(t, "shape") else None

    def __getitem__(self, key):
        if key == slice(None):
            return self.t[0:1, :]
        return self.t[key]


class _SubView:

    def __init__(self, t, c0):
        self.t = t
        self.c0 = c0

    def __getitem__(self, key):
        assert isinstance(key, tuple) and key[0] == slice(None)
        c = key[1]
        return self.t[:, self.c0 + (c.start or 0):self.c0 + c.stop]


def ntiles(total, tile):
    out = []
    o = 0
    while o < total:
        out.append((o, min(tile, total - o)))
        o += tile
    return out


def build_LA():
    TE = T + 3
    nc = bass.Bass("TRN2", target_bir_lowering=False)
    x0e = nc.dram_tensor("x0e", [D, TE], F32, kind="ExternalInput").ap()
    winT = nc.dram_tensor("winT", [D, 2 * DI], F32, kind="ExternalInput").ap()
    wxT = nc.dram_tensor("wxT", [128, 16 * NST], F32, kind="ExternalInput").ap()
    convw = nc.dram_tensor("convw", [128, 64], F32, kind="ExternalInput").ap()
    convb = nc.dram_tensor("convb", [128, 16], F32, kind="ExternalInput").ap()
    onesc = nc.dram_tensor("onesc", [128, 1], F32, kind="ExternalInput").ap()
    ones1 = nc.dram_tensor("ones1", [1, 128], F32, kind="ExternalInput").ap()

    xcs_d = nc.dram_tensor("xcs_d", [DI, T], F32, kind="ExternalOutput").ap()
    z_d = nc.dram_tensor("z_d", [DI, T], F32, kind="ExternalOutput").ap()
    sct_d = nc.dram_tensor("sct_d", [NST, T], F32, kind="ExternalOutput").ap()
    hend_d = nc.dram_tensor("hend_d", [128, 16], F32, kind="ExternalOutput").ap()

    NT = ntiles(TE, 512)
    NT_T = ntiles(T, 512)

    ctx = ExitStack()
    with ctx:
        sb = lambda name, shape, dt=F32: ctx.enter_context(nc.sbuf_tensor(name, shape, dt))
        win_bf = [sb(f"win{k}", [128, 2 * DI], BF16) for k in range(8)]
        wxt = sb("wxt", [128, 16 * NST], BF16)
        cw = sb("cw", [128, 64])
        cb = sb("cb", [128, 16])
        onc = sb("onc", [128, 1])
        on1 = sb("on1", [1, 128])
        x0s = [sb(f"x0s{i}", [128, 512]) for i in range(2)]
        sqs = [sb(f"sqs{i}", [128, 512]) for i in range(2)]
        sumx = sb("sumx", [1, TE])
        sumx2 = sb("sumx2", [1, TE])
        rtmp = sb("rtmp", [1, TE])
        rstd = sb("rstd", [1, TE])
        nmr = rtmp
        bc_r = sb("bc_r", [128, TE])
        bc_n = sb("bc_n", [128, TE])
        x0c = [sb("x0c0", [128, TE])] * 2
        xtmp = sb("xtmp", [128, TE])
        xn_bf = [sb(f"xnbf{k}", [128, TE], BF16) for k in range(8)]
        xz = [sb(f"xz{i}", [128, TE]) for i in range(2)]
        conva = xtmp
        xcs = sb("xcs", [128, T])
        xcsb = [sb(f"xcsb{i}", [128, T], BF16) for i in range(2)]
        h = bc_n
        a95 = bc_r
        hend = sb("hend", [128, 16])
        sct1 = sb("sct1", [NST, 512])
        psA = ctx.enter_context(nc.psum_tensor("psA", [128, 512], F32))
        psB = ctx.enter_context(nc.psum_tensor("psB", [128, 512], F32))
        psC = ctx.enter_context(nc.psum_tensor("psC", [NST, T], F32))

        p = Prog(nc, ctx)

        def f_wload(e, fin):
            for k in range(8):
                fin(e.dma_start(out=win_bf[k][:], in_=winT[k * 128:(k + 1) * 128, :]), 16)
            fin(e.dma_start(out=wxt[:], in_=wxT), 16)
        s_wload = p.step("gpsimd", 9 * 16, f_wload, name="wload")

        def f_cload(e, fin):
            for _rep in range(2):
                fin(e.dma_start(out=cw[:], in_=convw), 16)
                fin(e.dma_start(out=cb[:], in_=convb), 16)
                fin(e.dma_start(out=onc[:], in_=onesc), 16)
                fin(e.dma_start(out=on1[:], in_=ones1), 16)
        s_cload = p.step("sync", 128, f_cload, name="cload")

        ld_steps = {}
        sq_steps = {}
        mm_steps = {}
        ev_steps = {}
        for ni, (no, nw) in enumerate(NT):
            for k in range(8):
                par = k % 2
                it = (ni, k)
                prev = []
                key2 = (ni, k - 2) if k >= 2 else ((ni - 1, 6 + k) if ni else None)
                if key2 in mm_steps:
                    prev.append(mm_steps[key2])

                def f_ld(e, fin, k=k, no=no, nw=nw, par=par):
                    fin(e.dma_start(out=x0s[par][:, 0:nw],
                                    in_=x0e[k * 128:(k + 1) * 128, no:no + nw]), 16)
                ld_steps[it] = p.step("sync", 16, f_ld, after=prev, name=f"ldA{it}")

                def f_sq(e, fin, k=k, nw=nw, par=par):
                    fin(e.activation(out=sqs[par][:, 0:nw], in_=x0s[par][:, 0:nw],
                                     func=AF.Square))
                sq_steps[it] = p.step("scalar", 1, f_sq,
                                      after=[ld_steps[it]] + prev, name=f"sqA{it}")

                def f_mm(e, fin, k=k, nw=nw, par=par):
                    fin(e.matmul(psA[0:1, 0:nw], onc[:], x0s[par][:, 0:nw],
                                 start=(k == 0), stop=(k == 7)))
                    fin(e.matmul(psB[0:1, 0:nw], onc[:], sqs[par][:, 0:nw],
                                 start=(k == 0), stop=(k == 7)))
                mm_steps[it] = p.step(
                    "tensor", 2, f_mm,
                    after=[ld_steps[it], sq_steps[it], s_cload] +
                          ([ev_steps[ni - 1]] if (k == 0 and ni > 0) else []),
                    name=f"mmA{it}")

            def f_ev(e, fin, no=no, nw=nw):
                fin(e.copy(sumx[0:1, no:no + nw], psA[0:1, 0:nw]))
                fin(e.copy(sumx2[0:1, no:no + nw], psB[0:1, 0:nw]))
            ev_steps[ni] = p.step("scalar", 2, f_ev, after=[mm_steps[(ni, 7)]],
                                  name=f"evA{ni}")

        def f_rowmath(e, fin):
            fin(e.tensor_scalar_mul(sumx[:], sumx[:], 1.0 / D))
            fin(e.tensor_scalar_mul(sumx2[:], sumx2[:], 1.0 / D))
            fin(e.tensor_mul(rtmp[:], sumx[:], sumx[:]))
            fin(e.tensor_sub(sumx2[:], sumx2[:], rtmp[:]))
            fin(e.tensor_scalar_add(sumx2[:], sumx2[:], LN_EPS))
        s_rowmath = p.step("vector", 5, f_rowmath, fence=True,
                           after=[ev_steps[len(NT) - 1]], name="rowmath")

        def f_rstd(e, fin):
            fin(e.activation(out=rtmp[:], in_=sumx2[:], func=AF.Ln))
            fin(e.activation(out=rstd[:], in_=rtmp[:], func=AF.Exp, scale=-0.5))
        s_rstd = p.step("scalar", 2, f_rstd, after=[s_rowmath], name="rstd",
                        fence=True)

        def f_nmr(e, fin):
            fin(e.tensor_mul(nmr[:], sumx[:], rstd[:]))
            fin(e.tensor_scalar_mul(nmr[:], nmr[:], -1.0))
        s_nmr = p.step("vector", 2, f_nmr, after=[s_rstd], name="nmr",
                       fence=True)

        bc_ev = None
        for ni, (no, nw) in enumerate(NT):
            def f_bc(e, fin, no=no, nw=nw):
                fin(e.matmul(psA[:, 0:nw], on1[:], rstd[0:1, no:no + nw],
                             start=True, stop=True))
                fin(e.matmul(psB[:, 0:nw], on1[:], nmr[0:1, no:no + nw],
                             start=True, stop=True))
            s_bc = p.step("tensor", 2, f_bc,
                          after=[s_nmr] + ([bc_ev] if bc_ev else []),
                          name=f"bc{ni}")

            def f_bcev(e, fin, no=no, nw=nw):
                fin(e.copy(bc_r[:, no:no + nw], psA[:, 0:nw]))
                fin(e.copy(bc_n[:, no:no + nw], psB[:, 0:nw]))
            bc_ev = p.step("scalar", 2, f_bcev, after=[s_bc], name=f"bcev{ni}")

        xn_steps = {}
        ldb_steps = {}
        for k in range(8):
            par = k % 2
            prev = [xn_steps[k - 1]] if k >= 1 else []

            def f_ldb(e, fin, k=k, par=par):
                fin(e.dma_start(out=x0c[par][:], in_=x0e[k * 128:(k + 1) * 128, :]), 16)
            ldb_steps[k] = p.step("sync", 16, f_ldb, after=prev, name=f"ldB{k}")

            def f_xn(e, fin, k=k, par=par):
                fin(e.tensor_mul(xtmp[:], x0c[par][:], bc_r[:]))
                fin(e.tensor_add(xn_bf[k][:], xtmp[:], bc_n[:]))
            xn_steps[k] = p.step("vector", 2, f_xn, plug=True,
                                 after=[ldb_steps[k], bc_ev], name=f"xn{k}")
        s_xn_last = xn_steps[7]
        s_a95 = p.step("vector", 1,
                       lambda e, fin: fin(e.memset(a95[:, 0:T], 0.95)),
                       after=[s_xn_last], name="a95")

        mmx = {}
        evx = {}
        conv_s = {}
        silu_s = {}
        hcol_s = {}
        cast_s = {}
        scan_s = {}
        mmsc_s = {}
        xdma_s = {}
        zdma_s = {}
        for m in range(32):
            par = m % 2
            ps_m = psA if par == 0 else psB
            inner = m < 16
            for ni, (no, nw) in enumerate(NT):
                deps = [s_xn_last, s_wload]
                if ni > 0:
                    deps.append(evx[(m, ni - 1)])
                elif m >= 2:
                    deps.append(evx[(m - 2, len(NT) - 1)])
                if m < 2:
                    deps.append(bc_ev)

                def f_mmx(e, fin, m=m, no=no, nw=nw, ps_m=ps_m):
                    for k in range(8):
                        fin(e.matmul(ps_m[:, 0:nw],
                                     win_bf[k][:, m * 128:(m + 1) * 128],
                                     xn_bf[k][:, no:no + nw],
                                     start=(k == 0), stop=(k == 7)))
                mmx[(m, ni)] = p.step("tensor", 8, f_mmx, after=deps,
                                      name=f"mmx{m}_{ni}")

                evd = [mmx[(m, ni)]]
                if m >= 2:
                    evd.append(conv_s[m - 2] if (m - 2) < 16 else zdma_s[m - 2])

                def f_evx(e, fin, no=no, nw=nw, par=par, ps_m=ps_m):
                    fin(e.copy(xz[par][:, no:no + nw], ps_m[:, 0:nw]))
                evx[(m, ni)] = p.step("scalar", 1, f_evx, after=evd,
                                      name=f"evx{m}_{ni}")
            if inner:
                cdeps = [evx[(m, len(NT) - 1)], s_cload]
                if m >= 1:
                    cdeps.append(silu_s[m - 1])

                def f_conv(e, fin, m=m, par=par):
                    fin(e.tensor_scalar(conva[:, 0:T], xz[par][:, 0:T],
                                        cw[:, m * 4:m * 4 + 1], cb[:, m:m + 1],
                                        op0=ALU.mult, op1=ALU.add))
                    for j in range(1, 4):
                        fin(e.scalar_tensor_tensor(conva[:, 0:T], xz[par][:, j:j + T],
                                                   cw[:, m * 4 + j:m * 4 + j + 1],
                                                   conva[:, 0:T], op0=ALU.mult,
                                                   op1=ALU.add))
                conv_s[m] = p.step("vector", 4, f_conv, after=cdeps, plug=True,
                                   name=f"conv{m}")

                sdeps = [conv_s[m]]
                if m >= 1:
                    sdeps += [cast_s[m - 1], scan_s[m - 1], xdma_s[m - 1]]

                def f_silu(e, fin):
                    fin(e.activation(out=xcs[:], in_=conva[:, 0:T], func=AF.Silu))
                silu_s[m] = p.step("scalar", 1, f_silu, after=sdeps, plug=True,
                                   name=f"silu{m}")

                castd = [silu_s[m]]
                if m >= 2:
                    castd.append(mmsc_s[m - 2])

                def f_cast(e, fin, par=par):
                    fin(e.tensor_copy(xcsb[par][:], xcs[:]))
                cast_s[m] = p.step("vector", 1, f_cast, after=castd, plug=True,
                                   name=f"cast{m}")

                def f_scan(e, fin, m=m):
                    fin(e.tensor_tensor_scan(h[:, 0:T], a95[:, 0:T], xcs[:], 0.0,
                                             op0=ALU.mult, op1=ALU.add))
                sdeps2 = [silu_s[m], s_a95]
                if m >= 1:
                    sdeps2.append(hcol_s[m - 1])
                scan_s[m] = p.step("vector", 1, f_scan, plug=True,
                                   after=sdeps2, name=f"scan{m}")

                def f_hcol(e, fin, m=m):
                    fin(e.copy(hend[:, m:m + 1], h[:, T - 1:T]))
                hcol_s[m] = p.step("scalar", 1, f_hcol, after=[scan_s[m]], plug=True,
                                   name=f"hcol{m}")

                def f_mmsc(e, fin, m=m, par=par):
                    for ni2, (no2, nw2) in enumerate(NT_T):
                        fin(e.matmul(psC[:, no2:no2 + nw2],
                                     wxt[:, m * NST:(m + 1) * NST],
                                     xcsb[par][:, no2:no2 + nw2],
                                     start=(m == 0), stop=(m == 15)))
                mmsc_s[m] = p.step("tensor", 4, f_mmsc, after=[cast_s[m]],
                                   name=f"mmsc{m}")

                def f_xdma(e, fin, m=m):
                    fin(e.dma_start(out=xcs_d[m * 128:(m + 1) * 128, :],
                                    in_=xcs[:]), 16)
                xdma_s[m] = p.step("sync", 16, f_xdma, after=[silu_s[m]],
                                   name=f"xdma{m}")
            else:
                def f_zdma(e, fin, m=m, par=par):
                    fin(e.dma_start(out=z_d[(m - 16) * 128:(m - 15) * 128, :],
                                    in_=xz[par][:, 3:3 + T]), 16)
                zdma_s[m] = p.step("sync", 16, f_zdma,
                                   after=[evx[(m, len(NT) - 1)]],
                                   name=f"zdma{m}")

        s_sct = None
        for ni2, (no2, nw2) in enumerate(NT_T):
            def f_evsc(e, fin, no2=no2, nw2=nw2):
                fin(e.copy(sct1[:, 0:nw2], psC[:, no2:no2 + nw2]))
            s_evsc = p.step("scalar", 1, f_evsc,
                            after=[mmsc_s[15]] + ([s_sct] if s_sct else []),
                            name=f"evsc{ni2}")

            def f_sctdma(e, fin, no2=no2, nw2=nw2):
                fin(e.dma_start(out=sct_d[:, no2:no2 + nw2], in_=sct1[:, 0:nw2]), 16)
            s_sct = p.step("sync", 16, f_sctdma, after=[s_evsc], name=f"sctdma{ni2}")

        def f_hend(e, fin):
            fin(e.dma_start(out=hend_d, in_=hend[:]), 16)
        s_hend = p.step("sync", 16, f_hend, after=[hcol_s[15]], name="henddma")

        p.step("sync", 0, lambda e, fin: None, after=[s_hend], name="done")

        p.emit()
    return nc


def build_LB():
    nc = bass.Bass("TRN2", target_bir_lowering=False)
    xcs_d = nc.dram_tensor("xcs_d", [DI, T], F32, kind="ExternalInput").ap()
    z_d = nc.dram_tensor("z_d", [DI, T], F32, kind="ExternalInput").ap()
    sct_d = nc.dram_tensor("sct_d", [NST, T], F32, kind="ExternalInput").ap()
    carry_d = nc.dram_tensor("carry_d", [128, 16], F32, kind="ExternalInput").ap()
    x0T = nc.dram_tensor("x0T", [D, T], F32, kind="ExternalInput").ap()
    blogT = nc.dram_tensor("blogT", [NST, DI], F32, kind="ExternalInput").ap()
    woutT = nc.dram_tensor("woutT", [DI, D], F32, kind="ExternalInput").ap()
    lng = nc.dram_tensor("lng", [128, 8], F32, kind="ExternalInput").ap()
    lnb = nc.dram_tensor("lnb", [128, 8], F32, kind="ExternalInput").ap()
    onesc = nc.dram_tensor("onesc", [128, 1], F32, kind="ExternalInput").ap()
    ones1 = nc.dram_tensor("ones1", [1, 128], F32, kind="ExternalInput").ap()

    x1_d = nc.dram_tensor("x1_d", [D, T], F32, kind="ExternalOutput").ap()
    resid_d = nc.dram_tensor("resid_d", [D, T], F32, kind="Internal").ap()

    NT4 = ntiles(T, 512)

    ctx = ExitStack()
    with ctx:
        sb = lambda name, shape, dt=F32: ctx.enter_context(nc.sbuf_tensor(name, shape, dt))
        ybf = [sb(f"ybf{c}", [128, T], BF16) for c in range(16)]
        wpm = [sb(f"wpm{k}", [128, 128], BF16) for k in range(16)]
        sctt = sb("sctt", [NST, T])
        expb = sb("expb", [NST, DI])
        carr = sb("carr", [128, 16])
        g_t = sb("g_t", [128, 8])
        b_t = sb("b_t", [128, 8])
        onc = sb("onc", [128, 1])
        on1 = sb("on1", [1, 128])
        xcst = [sb(f"xcst{i}", [128, T]) for i in range(2)]
        zt = [sb(f"zt{i}", [128, T]) for i in range(2)]
        h = sb("h", [128, T])
        ytmp = sb("ytmp", [128, 512])
        a95 = sb("a95", [128, T])
        x0sub = [sb(f"x0sub{i}", [128, 512]) for i in range(2)]
        sqsub = [sb(f"sqsub{i}", [128, 512]) for i in range(2)]
        rows = {n: sb(f"row_{n}", [1, T]) for n in ("sumx", "sumx2", "rtmp", "rstd")}
        bc_r = sb("bc_r", [128, T])
        bc_n = sb("bc_n", [128, T])
        attn = a95
        resid = None
        x0c = sb("x0c", [128, T])
        psA = ctx.enter_context(nc.psum_tensor("psA", [128, 512], F32))
        psB = ctx.enter_context(nc.psum_tensor("psB", [128, 512], F32))

        p = Prog(nc, ctx)
        sumx, sumx2, rtmp, rstd = (rows["sumx"], rows["sumx2"],
                                   rows["rtmp"], rows["rstd"])
        nmr = rtmp

        def f_const(e, fin):
            for _rep in range(2):
                fin(e.dma_start(out=sctt[:], in_=sct_d), 16)
                fin(e.dma_start(out=carr[:], in_=carry_d), 16)
                fin(e.dma_start(out=g_t[:], in_=lng), 16)
                fin(e.dma_start(out=b_t[:], in_=lnb), 16)
                fin(e.dma_start(out=onc[:], in_=onesc), 16)
                fin(e.dma_start(out=on1[:], in_=ones1), 16)
                fin(e.dma_start(out=expb[:], in_=blogT), 16)
        s_const = p.step("sync", 14 * 16, f_const, name="const")

        def f_expb(e, fin):
            fin(e.activation(out=expb[:], in_=expb[:], func=AF.Exp))
        s_expb = p.step("scalar", 1, f_expb, after=[s_const], name="expb")

        s_a95 = p.step("vector", 1, lambda e, fin: fin(e.memset(a95[:], 0.95)),
                       name="a95")

        def stats_phase(tag, src_dram, nsrc_chunks, extra_dep):
            ld_s, sq_s, mm_s, ev_s = {}, {}, {}, {}
            for ni, (no, nw) in enumerate(NT4):
                for k in range(nsrc_chunks):
                    par = k % 2
                    it = (ni, k)
                    prev = list(extra_dep)
                    key2 = (ni, k - 2) if k >= 2 else ((ni - 1, nsrc_chunks - 2 + k) if ni else None)
                    if key2 in mm_s:
                        prev.append(mm_s[key2])

                    def f_ld(e, fin, k=k, no=no, nw=nw, par=par, src=src_dram):
                        fin(e.dma_start(out=x0sub[par][:, 0:nw],
                                        in_=src[k * 128:(k + 1) * 128, no:no + nw]), 16)
                    ld_s[it] = p.step("sync", 16, f_ld, after=prev, name=f"{tag}ld{it}")

                    def f_sq(e, fin, nw=nw, par=par):
                        fin(e.activation(out=sqsub[par][:, 0:nw],
                                         in_=x0sub[par][:, 0:nw], func=AF.Square))
                    sq_s[it] = p.step("scalar", 1, f_sq, after=[ld_s[it]] + prev,
                                      name=f"{tag}sq{it}")

                    def f_mm(e, fin, k=k, nw=nw, par=par, K=nsrc_chunks):
                        fin(e.matmul(psA[0:1, 0:nw], onc[:], x0sub[par][:, 0:nw],
                                     start=(k == 0), stop=(k == K - 1)))
                        fin(e.matmul(psB[0:1, 0:nw], onc[:], sqsub[par][:, 0:nw],
                                     start=(k == 0), stop=(k == K - 1)))
                    mm_s[it] = p.step(
                        "tensor", 2, f_mm,
                        after=[ld_s[it], sq_s[it]] +
                              ([ev_s[ni - 1]] if (k == 0 and ni > 0) else []),
                        name=f"{tag}mm{it}")

                def f_ev(e, fin, no=no, nw=nw):
                    fin(e.copy(sumx[0:1, no:no + nw], psA[0:1, 0:nw]))
                    fin(e.copy(sumx2[0:1, no:no + nw], psB[0:1, 0:nw]))
                ev_s[ni] = p.step("scalar", 2, f_ev,
                                  after=[mm_s[(ni, nsrc_chunks - 1)]],
                                  name=f"{tag}ev{ni}")

            nd = D if nsrc_chunks == 8 else 128 * nsrc_chunks

            def f_rm(e, fin, nd=nd):
                fin(e.tensor_scalar_mul(sumx[:], sumx[:], 1.0 / nd))
                fin(e.tensor_scalar_mul(sumx2[:], sumx2[:], 1.0 / nd))
                fin(e.tensor_mul(rtmp[:], sumx[:], sumx[:]))
                fin(e.tensor_sub(sumx2[:], sumx2[:], rtmp[:]))
                fin(e.tensor_scalar_add(sumx2[:], sumx2[:], LN_EPS))
            s_rm = p.step("vector", 5, f_rm, after=[ev_s[len(NT4) - 1]],
                          name=f"{tag}rm", fence=True)

            def f_rs(e, fin):
                fin(e.activation(out=rtmp[:], in_=sumx2[:], func=AF.Ln))
                fin(e.activation(out=rstd[:], in_=rtmp[:], func=AF.Exp, scale=-0.5))
            s_rs = p.step("scalar", 2, f_rs, after=[s_rm], name=f"{tag}rs",
                          fence=True)

            def f_nm(e, fin):
                fin(e.tensor_mul(nmr[:], sumx[:], rstd[:]))
                fin(e.tensor_scalar_mul(nmr[:], nmr[:], -1.0))
            s_nm = p.step("vector", 2, f_nm, after=[s_rs], name=f"{tag}nm",
                          fence=True)

            bc_last = None
            for ni, (no, nw) in enumerate(NT4):
                def f_bc(e, fin, no=no, nw=nw):
                    fin(e.matmul(psA[:, 0:nw], on1[:], rstd[0:1, no:no + nw],
                                 start=True, stop=True))
                    fin(e.matmul(psB[:, 0:nw], on1[:], nmr[0:1, no:no + nw],
                                 start=True, stop=True))
                s_bc = p.step("tensor", 2, f_bc,
                              after=[s_nm] + ([bc_last] if bc_last else []),
                              name=f"{tag}bc{ni}")

                def f_be(e, fin, no=no, nw=nw):
                    fin(e.copy(bc_r[:, no:no + nw], psA[:, 0:nw]))
                    fin(e.copy(bc_n[:, no:no + nw], psB[:, 0:nw]))
                bc_last = p.step("scalar", 2, f_be, after=[s_bc], name=f"{tag}be{ni}")
            return bc_last

        s_bc1 = stats_phase("s1", x0T, 8, [s_const])

        scan_s, ymul_s, silu_s, ldx_s, ldz_s, ctmm_s = {}, {}, {}, {}, {}, {}
        for c in range(16):
            par = c % 2
            dep_ld = [ymul_s[(c - 2, 3)]] if c >= 2 else []

            def f_ldx(e, fin, c=c, par=par):
                fin(e.dma_start(out=xcst[par][:], in_=xcs_d[c * 128:(c + 1) * 128, :]), 16)
            ldx_s[c] = p.step("sync", 16, f_ldx, after=dep_ld, name=f"ldx{c}")

            dep_lz = [ymul_s[(c - 2, 3)]] if c >= 2 else []

            def f_ldz(e, fin, c=c, par=par):
                fin(e.dma_start(out=zt[par][:], in_=z_d[c * 128:(c + 1) * 128, :]), 16)
            ldz_s[c] = p.step("sync", 16, f_ldz, after=dep_lz, name=f"ldz{c}")

            sdep = [ldx_s[c], s_a95, s_const]
            if c >= 1:
                sdep.append(ymul_s[(c - 1, 3)])

            def f_scan(e, fin, c=c, par=par):
                fin(e.tensor_tensor_scan(h[:], a95[:], xcst[par][:],
                                         carr[:, c:c + 1], op0=ALU.mult,
                                         op1=ALU.add))
            scan_s[c] = p.step("vector", 1, f_scan, after=sdep, plug=True, name=f"scan{c}")

            sldep = [ldz_s[c]]
            if c >= 1:
                sldep.append(ymul_s[(c - 1, 3)])

            def f_silu(e, fin, par=par):
                fin(e.activation(out=zt[par][:], in_=zt[par][:], func=AF.Silu))
            silu_s[c] = p.step("scalar", 1, f_silu, after=sldep, plug=True, name=f"sz{c}")

            for nt, (no, nw) in enumerate(NT4):
                ps_n = psA if nt % 2 == 0 else psB
                mdep = [s_expb, s_const]
                if (c, nt - 2) in ymul_s:
                    mdep.append(ymul_s[(c, nt - 2)])
                elif c >= 1 and (c - 1, nt + 2) in ymul_s:
                    mdep.append(ymul_s[(c - 1, nt + 2)])
                if c == 0 and nt < 2:
                    mdep.append(s_bc1)

                def f_ct(e, fin, c=c, no=no, nw=nw, ps_n=ps_n):
                    fin(e.matmul(ps_n[:, 0:nw], expb[:, c * 128:(c + 1) * 128],
                                 sctt[:, no:no + nw], start=True, stop=True))
                ctmm_s[(c, nt)] = p.step("tensor", 1, f_ct, after=mdep,
                                         name=f"ct{c}_{nt}")

                def f_ym(e, fin, c=c, nt=nt, no=no, nw=nw, ps_n=ps_n, par=par):
                    fin(e.tensor_mul(ytmp[:, 0:nw], h[:, no:no + nw], ps_n[:, 0:nw]))
                    fin(e.tensor_mul(ybf[c][:, no:no + nw], ytmp[:, 0:nw],
                                     zt[par][:, no:no + nw]))
                ymul_s[(c, nt)] = p.step(
                    "vector", 2, f_ym, fence=True, plug=True,
                    after=[scan_s[c], silu_s[c], ctmm_s[(c, nt)]],
                    name=f"ym{c}_{nt}")

        evw_s, rdma_s, wmm_s = {}, {}, {}
        last_ym = ymul_s[(15, 3)]
        for m in range(8):
            wld_dep = [wmm_s[(m - 1, 3)]] if m >= 1 else []

            def f_wpml(e, fin, m=m):
                for _rep in range(2):
                    for k in range(16):
                        fin(e.dma_start(out=wpm[k][:],
                                        in_=woutT[k * 128:(k + 1) * 128,
                                                  m * 128:(m + 1) * 128]), 16)
            s_wpml = p.step("gpsimd", 2 * 16 * 16, f_wpml, after=wld_dep,
                            name=f"wpml{m}")
            for nt, (no, nw) in enumerate(NT4):
                ps_n = psA if nt % 2 == 0 else psB
                mdep = [last_ym, s_wpml]
                if (m, nt - 2) in evw_s:
                    mdep.append(evw_s[(m, nt - 2)])
                elif m >= 1 and (m - 1, nt + 2) in evw_s:
                    mdep.append(evw_s[(m - 1, nt + 2)])

                def f_wmm(e, fin, m=m, no=no, nw=nw, ps_n=ps_n):
                    for k in range(16):
                        fin(e.matmul(ps_n[:, 0:nw], wpm[k][:],
                                     ybf[k][:, no:no + nw],
                                     start=(k == 0), stop=(k == 15)))
                s_wmm = p.step("tensor", 16, f_wmm, after=mdep, name=f"wmm{m}_{nt}")
                wmm_s[(m, nt)] = s_wmm

                evd = [s_wmm]
                if nt == 0:
                    evd.append(rdma_s[m - 1] if m >= 1 else last_ym)

                def f_evw(e, fin, no=no, nw=nw, ps_n=ps_n):
                    fin(e.copy(attn[:, no:no + nw], ps_n[:, 0:nw]))
                evw_s[(m, nt)] = p.step("scalar", 1, f_evw, after=evd,
                                        name=f"evw{m}_{nt}")

            def f_ldx0(e, fin, m=m):
                fin(e.dma_start(out=x0c[:], in_=x0T[m * 128:(m + 1) * 128, :]), 16)
            s_ldx0 = p.step("sync", 16, f_ldx0,
                            after=[rdma_s[m - 1]] if m >= 1 else [s_bc1],
                            name=f"ldx0_{m}")

            def f_res(e, fin, m=m):
                fin(e.tensor_mul(x0c[:], x0c[:], bc_r[:]))
                fin(e.tensor_add(x0c[:], x0c[:], bc_n[:]))
                fin(e.tensor_add(x0c[:], x0c[:], attn[:]))
            s_res = p.step("vector", 3, f_res, plug=True,
                           after=[s_ldx0, evw_s[(m, 3)]], name=f"res{m}")

            def f_rdma(e, fin, m=m):
                fin(e.dma_start(out=resid_d[m * 128:(m + 1) * 128, :], in_=x0c[:]), 16)
            rdma_s[m] = p.step("sync", 16, f_rdma, after=[s_res], name=f"rdma{m}")

        s_bc2 = stats_phase("s2", resid_d, 8, [rdma_s[7]])

        last_o = None
        for m in range(8):
            dep0 = [s_bc2] + ([last_o] if last_o else [])

            def f_ldr(e, fin, m=m):
                fin(e.dma_start(out=attn[:], in_=resid_d[m * 128:(m + 1) * 128, :]), 16)
                fin(e.dma_start(out=x0c[:], in_=x0T[m * 128:(m + 1) * 128, :]), 16)
            s_ldr = p.step("sync", 32, f_ldr, after=dep0, name=f"ldr{m}")

            def f_fin(e, fin, m=m):
                fin(e.tensor_mul(attn[:], attn[:], bc_r[:]))
                fin(e.tensor_add(attn[:], attn[:], bc_n[:]))
                fin(e.tensor_scalar(attn[:], attn[:], g_t[:, m:m + 1],
                                    b_t[:, m:m + 1], op0=ALU.mult, op1=ALU.add))
                fin(e.tensor_add(attn[:], attn[:], x0c[:]))
            s_fin = p.step("vector", 4, f_fin, after=[s_ldr], plug=True, name=f"fin{m}")

            def f_odma(e, fin, m=m):
                fin(e.dma_start(out=x1_d[m * 128:(m + 1) * 128, :], in_=attn[:]), 16)
            last_o = p.step("sync", 16, f_odma, after=[s_fin], name=f"odma{m}")

        p.step("sync", 0, lambda e, fin: None, after=[last_o], name="done")
        p.emit()
    return nc


def build_LC():
    nc = bass.Bass("TRN2", target_bir_lowering=False)
    x1T = nc.dram_tensor("x1T", [D, T], F32, kind="ExternalInput").ap()
    wr = nc.dram_tensor("wr", [D, 8], F32, kind="ExternalInput").ap()
    iota8c = nc.dram_tensor("iota8c", [128, 8], F32, kind="ExternalInput").ap()
    eye128 = nc.dram_tensor("eye128", [128, 128], F32, kind="ExternalInput").ap()
    onesc = nc.dram_tensor("onesc", [128, 1], F32, kind="ExternalInput").ap()
    ones1 = nc.dram_tensor("ones1", [1, 128], F32, kind="ExternalInput").ap()
    rowbounce = nc.dram_tensor("rowbounce", [2, T], F32, kind="Internal").ap()

    xn1_d = nc.dram_tensor("xn1_d", [T, D], F32, kind="ExternalOutput").ap()
    e1_d = nc.dram_tensor("e1_d", [128, 16], F32, kind="ExternalOutput").ap()
    e2_d = nc.dram_tensor("e2_d", [128, 16], F32, kind="ExternalOutput").ap()
    g1_d = nc.dram_tensor("g1_d", [128, 16], F32, kind="ExternalOutput").ap()
    g2_d = nc.dram_tensor("g2_d", [128, 16], F32, kind="ExternalOutput").ap()

    NT4 = ntiles(T, 512)

    ctx = ExitStack()
    with ctx:
        sb = lambda name, shape, dt=F32: ctx.enter_context(nc.sbuf_tensor(name, shape, dt))
        xn_bf = [sb(f"xnbf{k}", [128, T], BF16) for k in range(8)]
        wrb = sb("wrb", [128, 8 * 8], BF16)
        iot8 = sb("iot8", [128, 8])
        eight = sb("eight", [128, 8])
        eye = sb("eye", [128, 128])
        onc = sb("onc", [128, 1])
        on1 = sb("on1", [1, 128])
        x0sub = [sb(f"x0sub{i}", [128, 512]) for i in range(2)]
        sqsub = [sb(f"sqsub{i}", [128, 512]) for i in range(2)]
        sumx = sb("sumx", [1, T])
        sumx2 = sb("sumx2", [1, T])
        rtmp = sb("rtmp", [1, T])
        rstd = sb("rstd", [1, T])
        nmr = rtmp
        bc_r = sb("bc_r", [128, T])
        bc_n = sb("bc_n", [128, T])
        x1c = sb("x1c", [128, T])
        xtmp = sb("xtmp", [128, T])
        rcol = sb("rcol", [128, 16])
        ncol = sb("ncol", [128, 16])
        tmt = [sb(f"tmt{i}", [128, D]) for i in range(2)]
        scoresf = sb("scoresf", [8, T])
        sc_tm = sb("sc_tm", [128, 8])
        prb = sb("prb", [128, 8])
        red1 = sb("red1", [128, 1])
        red2 = sb("red2", [128, 1])
        msk = sb("msk", [128, 8])
        msku = sb("msku", [128, 8], U32)
        cand = sb("cand", [128, 8])
        e1t = sb("e1t", [128, 16])
        e2t = sb("e2t", [128, 16])
        g1t = sb("g1t", [128, 16])
        g2t = sb("g2t", [128, 16])
        psA = ctx.enter_context(nc.psum_tensor("psA", [128, 512], F32))
        psB = ctx.enter_context(nc.psum_tensor("psB", [128, 512], F32))
        psS = ctx.enter_context(nc.psum_tensor("psS", [8, T], F32))
        psT = ctx.enter_context(nc.psum_tensor("psT", [128, 128], F32))

        p = Prog(nc, ctx)

        def f_const(e, fin):
            for _rep in range(2):
                fin(e.dma_start(out=iot8[:], in_=iota8c), 16)
                fin(e.dma_start(out=eye[:], in_=eye128), 16)
                fin(e.dma_start(out=onc[:], in_=onesc), 16)
                fin(e.dma_start(out=on1[:], in_=ones1), 16)
        s_const = p.step("sync", 128, f_const, name="const")

        s_eight = p.step("vector", 1, lambda e, fin: fin(e.memset(eight[:], 8.0)),
                         name="eight")

        def f_wload(e, fin):
            for k in range(8):
                fin(e.dma_start(out=wrb[:, k * 8:(k + 1) * 8],
                                in_=wr[k * 128:(k + 1) * 128, :]), 16)
        s_wload = p.step("gpsimd", 128, f_wload, name="wload")

        ld_s, sq_s, mm_s, ev_s = {}, {}, {}, {}
        for ni, (no, nw) in enumerate(NT4):
            for k in range(8):
                par = k % 2
                it = (ni, k)
                prev = [s_const]
                key2 = (ni, k - 2) if k >= 2 else ((ni - 1, 6 + k) if ni else None)
                if key2 in mm_s:
                    prev.append(mm_s[key2])

                def f_ld(e, fin, k=k, no=no, nw=nw, par=par):
                    fin(e.dma_start(out=x0sub[par][:, 0:nw],
                                    in_=x1T[k * 128:(k + 1) * 128, no:no + nw]), 16)
                ld_s[it] = p.step("sync", 16, f_ld, after=prev, name=f"ld{it}")

                def f_sq(e, fin, nw=nw, par=par):
                    fin(e.activation(out=sqsub[par][:, 0:nw],
                                     in_=x0sub[par][:, 0:nw], func=AF.Square))
                sq_s[it] = p.step("scalar", 1, f_sq, after=[ld_s[it]] + prev,
                                  name=f"sq{it}")

                def f_mm(e, fin, k=k, nw=nw, par=par):
                    fin(e.matmul(psA[0:1, 0:nw], onc[:], x0sub[par][:, 0:nw],
                                 start=(k == 0), stop=(k == 7)))
                    fin(e.matmul(psB[0:1, 0:nw], onc[:], sqsub[par][:, 0:nw],
                                 start=(k == 0), stop=(k == 7)))
                mm_s[it] = p.step(
                    "tensor", 2, f_mm,
                    after=[ld_s[it], sq_s[it]] +
                          ([ev_s[ni - 1]] if (k == 0 and ni > 0) else []),
                    name=f"mm{it}")

            def f_ev(e, fin, no=no, nw=nw):
                fin(e.copy(sumx[0:1, no:no + nw], psA[0:1, 0:nw]))
                fin(e.copy(sumx2[0:1, no:no + nw], psB[0:1, 0:nw]))
            ev_s[ni] = p.step("scalar", 2, f_ev, after=[mm_s[(ni, 7)]],
                              name=f"ev{ni}")

        def f_rm(e, fin):
            fin(e.tensor_scalar_mul(sumx[:], sumx[:], 1.0 / D))
            fin(e.tensor_scalar_mul(sumx2[:], sumx2[:], 1.0 / D))
            fin(e.tensor_mul(rtmp[:], sumx[:], sumx[:]))
            fin(e.tensor_sub(sumx2[:], sumx2[:], rtmp[:]))
            fin(e.tensor_scalar_add(sumx2[:], sumx2[:], LN_EPS))
        s_rm = p.step("vector", 5, f_rm, after=[ev_s[3]], name="rm", fence=True)

        def f_rs(e, fin):
            fin(e.activation(out=rtmp[:], in_=sumx2[:], func=AF.Ln))
            fin(e.activation(out=rstd[:], in_=rtmp[:], func=AF.Exp, scale=-0.5))
        s_rs = p.step("scalar", 2, f_rs, after=[s_rm], name="rs", fence=True)

        def f_nm(e, fin):
            fin(e.tensor_mul(nmr[:], sumx[:], rstd[:]))
            fin(e.tensor_scalar_mul(nmr[:], nmr[:], -1.0))
        s_nm = p.step("vector", 2, f_nm, after=[s_rs], name="nm", fence=True)

        def f_rowout(e, fin):
            fin(e.dma_start(out=rowbounce[0:1, :], in_=rstd[:]), 16)
            fin(e.dma_start(out=rowbounce[1:2, :], in_=nmr[:]), 16)
        s_rowout = p.step("sync", 32, f_rowout, after=[s_nm], name="rowout")

        def f_rowin(e, fin):
            fin(e.dma_start(out=rcol[:],
                            in_=rowbounce[0:1, :].rearrange("o (a b) -> (o b) a", b=128),
                            allow_slow_non_contiguous=True), 16)
            fin(e.dma_start(out=ncol[:],
                            in_=rowbounce[1:2, :].rearrange("o (a b) -> (o b) a", b=128),
                            allow_slow_non_contiguous=True), 16)
            fin(e.dma_start(out=rcol[:],
                            in_=rowbounce[0:1, :].rearrange("o (a b) -> (o b) a", b=128),
                            allow_slow_non_contiguous=True), 16)
            fin(e.dma_start(out=ncol[:],
                            in_=rowbounce[1:2, :].rearrange("o (a b) -> (o b) a", b=128),
                            allow_slow_non_contiguous=True), 16)
        s_rowin = p.step("sync", 64, f_rowin, after=[s_rowout], name="rowin")

        bc_last = None
        for ni, (no, nw) in enumerate(NT4):
            def f_bc(e, fin, no=no, nw=nw):
                fin(e.matmul(psA[:, 0:nw], on1[:], rstd[0:1, no:no + nw],
                             start=True, stop=True))
                fin(e.matmul(psB[:, 0:nw], on1[:], nmr[0:1, no:no + nw],
                             start=True, stop=True))
            s_bc = p.step("tensor", 2, f_bc,
                          after=[s_nm] + ([bc_last] if bc_last else []),
                          name=f"bc{ni}")

            def f_be(e, fin, no=no, nw=nw):
                fin(e.copy(bc_r[:, no:no + nw], psA[:, 0:nw]))
                fin(e.copy(bc_n[:, no:no + nw], psB[:, 0:nw]))
            bc_last = p.step("scalar", 2, f_be, after=[s_bc], name=f"be{ni}")

        xn_s = {}
        ldc_s = {}
        for k in range(8):
            def f_ldc(e, fin, k=k):
                fin(e.dma_start(out=x1c[:], in_=x1T[k * 128:(k + 1) * 128, :]), 16)
            ldc_s[k] = p.step("sync", 16, f_ldc,
                              after=[xn_s[k - 1]] if k >= 1 else [bc_last],
                              name=f"ldc{k}")

            def f_xn(e, fin, k=k):
                fin(e.tensor_mul(xtmp[:], x1c[:], bc_r[:]))
                fin(e.tensor_add(xn_bf[k][:], xtmp[:], bc_n[:]))
            xn_s[k] = p.step("vector", 2, f_xn, after=[ldc_s[k], bc_last], plug=True,
                             name=f"xn{k}")
        s_xn_last = xn_s[7]

        tp_s = {}
        nrm_s = {}
        tdma_s = {}
        for tc in range(16):
            par = tc % 2
            for m in range(8):
                dep = [s_const, s_xn_last]
                if (tc, m - 1) in nrm_s:
                    dep.append(nrm_s[(tc, m - 1)])
                elif tc >= 1:
                    dep.append(nrm_s[(tc - 1, 7)])
                if tc >= 2:
                    dep.append(tdma_s[tc - 2])

                def f_ldt(e, fin, m=m, tc=tc):
                    fin(e.dma_start(out=x1c[:, 0:128],
                                    in_=x1T[m * 128:(m + 1) * 128,
                                            tc * 128:(tc + 1) * 128]), 16)
                s_ldt = p.step("scalar", 16, f_ldt, after=dep, name=f"ldt{tc}_{m}")

                def f_tp(e, fin, m=m, tc=tc):
                    fin(e.transpose(psT[:], x1c[:, 0:128], eye[:]))
                s_tp = p.step("tensor", 1, f_tp, after=[s_ldt], name=f"tp{tc}_{m}")

                def f_nrm(e, fin, m=m, tc=tc, par=par):
                    fin(e.tensor_scalar(tmt[par][:, m * 128:(m + 1) * 128],
                                        psT[:], rcol[:, tc:tc + 1],
                                        ncol[:, tc:tc + 1],
                                        op0=ALU.mult, op1=ALU.add))
                nrm_s[(tc, m)] = p.step("vector", 1, f_nrm, plug=True,
                                        after=[s_tp, s_rowin], name=f"nrm{tc}_{m}")

            def f_tdma(e, fin, tc=tc, par=par):
                fin(e.dma_start(out=xn1_d[tc * 128:(tc + 1) * 128, :],
                                in_=tmt[par][:]), 16)
            tdma_s[tc] = p.step("sync", 16, f_tdma, after=[nrm_s[(tc, 7)]],
                                name=f"tdma{tc}")

        def f_smm(e, fin):
            for nt, (no, nw) in enumerate(NT4):
                for k in range(8):
                    fin(e.matmul(psS[:, no:no + nw], wrb[:, k * 8:(k + 1) * 8],
                                 xn_bf[k][:, no:no + nw],
                                 start=(k == 0), stop=(k == 7)))
        s_smm = p.step("tensor", 32, f_smm, after=[s_xn_last, s_wload],
                       name="smm")

        def f_sev(e, fin):
            for nt, (no, nw) in enumerate(NT4):
                fin(e.copy(scoresf[:, no:no + nw], psS[:, no:no + nw]))
        s_sev = p.step("scalar", 4, f_sev, after=[s_smm], name="sev")

        prev_tc = None
        for tc in range(16):
            def f_stp(e, fin, tc=tc):
                fin(e.transpose(psT[0:128, 0:8], scoresf[:, tc * 128:(tc + 1) * 128],
                                eye[0:8, 0:8]))
            dep = [s_sev] + ([prev_tc] if prev_tc else [nrm_s[(15, 7)]])
            s_stp = p.step("tensor", 1, f_stp, after=dep, name=f"stp{tc}")

            def f_sme(e, fin, tc=tc):
                fin(e.tensor_copy(sc_tm[:], psT[0:128, 0:8]))
            s_sme = p.step("vector", 1, f_sme, after=[s_stp], name=f"sme{tc}",
                           fence=True)

            def f_soft(e, fin, tc=tc):
                fin(e.reduce_max(red1[:], sc_tm[:], axis=AX.X, negate=True))
            s_soft = p.step("vector", 1, f_soft, after=[s_sme], name=f"soft{tc}",
                            fence=True)

            def f_exp(e, fin, tc=tc):
                fin(e.activation(out=prb[:], in_=sc_tm[:], func=AF.Exp,
                                 bias=red1[:, 0:1]))
            s_exp = p.step("scalar", 1, f_exp, after=[s_soft], name=f"exp{tc}")

            def f_top(e, fin, tc=tc):
                b8 = lambda col: col.broadcast_to((128, 8))
                fin(e.reduce_sum(red2[:], prb[:], axis=AX.X))
                fin(e.reciprocal(red2[:], red2[:]))
                fin(e.tensor_mul(prb[:], prb[:], b8(red2[:, 0:1])))
                fin(e.reduce_max(g1t[:, tc:tc + 1], prb[:], axis=AX.X))
                fin(e.tensor_tensor(msku[:], prb[:], b8(g1t[:, tc:tc + 1]),
                                    op=ALU.is_equal))
                fin(e.select(cand[:], msku[:], iot8[:], eight[:]))
                fin(e.tensor_reduce(e1t[:, tc:tc + 1], cand[:], axis=AX.X,
                                    op=ALU.min))
                fin(e.tensor_tensor(msk[:], iot8[:], b8(e1t[:, tc:tc + 1]),
                                    op=ALU.is_equal))
                fin(e.scalar_tensor_tensor(msk[:], msk[:], -1e30, prb[:],
                                           op0=ALU.mult, op1=ALU.add))
                fin(e.reduce_max(g2t[:, tc:tc + 1], msk[:], axis=AX.X))
                fin(e.tensor_tensor(msku[:], msk[:], b8(g2t[:, tc:tc + 1]),
                                    op=ALU.is_equal))
                fin(e.select(cand[:], msku[:], iot8[:], eight[:]))
                fin(e.tensor_reduce(e2t[:, tc:tc + 1], cand[:], axis=AX.X,
                                    op=ALU.min))
            prev_tc = p.step("vector", 13, f_top, after=[s_exp, s_eight, s_const],
                             name=f"top{tc}", fence=True)

        def f_edma(e, fin):
            fin(e.dma_start(out=e1_d, in_=e1t[:]), 16)
            fin(e.dma_start(out=e2_d, in_=e2t[:]), 16)
            fin(e.dma_start(out=g1_d, in_=g1t[:]), 16)
            fin(e.dma_start(out=g2_d, in_=g2t[:]), 16)
        s_edma = p.step("sync", 64, f_edma, after=[prev_tc], name="edma")

        p.step("sync", 0, lambda e, fin: None, after=[s_edma], name="done")
        p.emit()
    return nc


def build_LD1():
    CAP = 4096
    NA = 32768
    QW = 2048
    BIGF = 1.0e9
    nc = bass.Bass("TRN2", target_bir_lowering=False)
    eflat = nc.dram_tensor("eflat", [1, NA], F32, kind="ExternalInput").ap()
    iota8r = nc.dram_tensor("iota8r", [8, 1], F32, kind="ExternalInput").ap()
    ones8 = nc.dram_tensor("ones8", [8, 1], F32, kind="ExternalInput").ap()
    tokmap = nc.dram_tensor("tokmap", [128, 64], U32, kind="ExternalInput").ap()
    mycid = nc.dram_tensor("mycid", [1, 1], U32, kind="ExternalInput").ap()

    slotv_d = nc.dram_tensor("slotv_d", [128, 256], U32, kind="ExternalOutput").ap()
    s2tp_d = nc.dram_tensor("s2tp_d", [8 * CAP, 2], U32, kind="ExternalOutput").ap()
    cumsel_d = nc.dram_tensor("cumsel_d", [1, NA], F32, kind="Internal").ap()
    sid_d = nc.dram_tensor("sid_d", [128, 256], U32, kind="Internal").ap()

    ctx = ExitStack()
    with ctx:
        sb = lambda name, shape, dt=F32: ctx.enter_context(nc.sbuf_tensor(name, shape, dt))
        arep = sb("arep", [8, QW])
        mtile = sb("mtile", [8, QW])
        onest = sb("onest", [8, QW])
        cumt = sb("cumt", [8, QW])
        state8 = sb("state8", [8, 1])
        io8 = sb("io8", [8, 1])
        on8 = sb("on8", [8, 1])
        cumrow = sb("cumrow", [1, QW])
        cself = sb("cself", [128, 256])
        eselt = sb("eselt", [128, 256])
        rankt = sb("rankt", [128, 256])
        vmask = sb("vmask", [128, 256])
        bigt = sb("bigt", [128, 256])
        svf = sb("svf", [128, 256])
        masku = sb("masku", [128, 256], U32)
        slotu = sb("slotu", [128, 256], U32)
        qt = sb("qt", [128, 256])
        pmt = sb("pmt", [128, 256])
        sidu = sb("sidu", [128, 256], U32)
        myslot = sb("myslot", [128, 32], U32)
        tokt = sb("tokt", [128, 64], U32)
        bigu = sb("bigu", [128, 128], U32)
        ps = [ctx.enter_context(nc.psum_tensor(f"ps{i}", [1, 512], F32))
              for i in range(2)]

        p = Prog(nc, ctx)

        def f_const(e, fin):
            for _rep in range(2):
                fin(e.dma_start(out=io8[:], in_=iota8r), 16)
                fin(e.dma_start(out=on8[:], in_=ones8), 16)
                fin(e.dma_start(out=tokt[:], in_=tokmap), 16)
        s_const = p.step("sync", 96, f_const, name="const")

        def f_init(e, fin):
            fin(e.memset(onest[:], 1.0))
            fin(e.memset(state8[:], 0.0))
            fin(e.memset(bigt[:], BIGF))
            fin(e.memset(bigu[:], 2_000_000_000))
        s_init = p.step("vector", 4, f_init, name="init")

        r2_s, r3_s, r4_s, r7_s = {}, {}, {}, {}
        for q in range(NA // QW):
            def f_r1(e, fin, q=q):
                fin(e.dma_start(out=arep[:],
                                in_=eflat[0:1, q * QW:(q + 1) * QW].broadcast_to((8, QW))),
                    16)
                fin(e.dma_start(out=arep[:],
                                in_=eflat[0:1, q * QW:(q + 1) * QW].broadcast_to((8, QW))),
                    16)
            dep = [r4_s[q - 1], r3_s[q - 1]] if q >= 1 else [s_const]
            r1 = p.step("sync", 32, f_r1, after=dep, name=f"r1_{q}")

            def f_r2(e, fin, q=q):
                fin(e.tensor_scalar(mtile[:], arep[:], io8[:, 0:1], None,
                                    op0=ALU.is_equal))
                fin(e.tensor_tensor_scan(cumt[:], onest[:], mtile[:],
                                         state8[:, 0:1], op0=ALU.mult, op1=ALU.add))
            r2_s[q] = p.step("vector", 2, f_r2, fence=True, plug=True,
                             after=[r1, s_init] + ([r3_s[q - 1]] if q else []),
                             name=f"r2_{q}")

            def f_r3(e, fin, q=q):
                fin(e.copy(state8[:], cumt[:, QW - 1:QW]))
            r3_s[q] = p.step("scalar", 1, f_r3, after=[r2_s[q]], plug=True, name=f"r3_{q}")

            def f_r4(e, fin, q=q):
                fin(e.tensor_mul(mtile[:], mtile[:], cumt[:]))
            r4_s[q] = p.step("vector", 1, f_r4, fence=True, plug=True, after=[r2_s[q]],
                             name=f"r4_{q}")

            ev_prev = None
            for nt, (no, nw) in enumerate(ntiles(QW, 512)):
                psn = ps[nt % 2]

                def f_r5(e, fin, no=no, nw=nw, psn=psn):
                    fin(e.matmul(psn[0:1, 0:nw], on8[:], mtile[:, no:no + nw],
                                 start=True, stop=True))
                dep5 = [r4_s[q], s_const]
                if ev_prev is not None:
                    dep5.append(ev_prev)
                elif q >= 1:
                    dep5.append(r7_s[q - 1])
                s_r5 = p.step("tensor", 1, f_r5, after=dep5, name=f"r5_{q}_{nt}")

                def f_r6(e, fin, no=no, nw=nw, psn=psn):
                    fin(e.copy(cumrow[0:1, no:no + nw], psn[0:1, 0:nw]))
                ev_prev = p.step("scalar", 1, f_r6, after=[s_r5], name=f"r6_{q}_{nt}")

            def f_r7(e, fin, q=q):
                fin(e.dma_start(out=cumsel_d[0:1, q * QW:(q + 1) * QW],
                                in_=cumrow[:]), 16)
            r7_s[q] = p.step("sync", 16, f_r7, after=[ev_prev], name=f"r7_{q}")

        def f_r8(e, fin):
            for _rep in range(2):
                fin(e.dma_start(out=cself[:],
                                in_=cumsel_d.rearrange("o (j p) -> (o p) j", p=128),
                                allow_slow_non_contiguous=True), 16)
                fin(e.dma_start(out=eselt[:],
                                in_=eflat.rearrange("o (j p) -> (o p) j", p=128),
                                allow_slow_non_contiguous=True), 16)
        s_r8 = p.step("sync", 64, f_r8, after=[r7_s[NA // QW - 1]], name="r8")

        MAGIC = 12582912.0

        def f_r9(e, fin):
            fin(e.tensor_scalar_add(rankt[:], cself[:], -1.0))
            fin(e.tensor_single_scalar(vmask[:], rankt[:], 4096.0 - 0.5, ALU.is_lt))
            fin(e.scalar_tensor_tensor(svf[:], eselt[:], 4096.0, rankt[:],
                                       op0=ALU.mult, op1=ALU.add))
            fin(e.tensor_copy(masku[:], vmask[:]))
            fin(e.select(slotu[:], masku[:], svf[:], bigt[:]))
            fin(e.tensor_scalar(qt[:], rankt[:], 1.0 / 128.0, -0.4999,
                                op0=ALU.mult, op1=ALU.add))
            fin(e.tensor_scalar(qt[:], qt[:], MAGIC, -MAGIC,
                                op0=ALU.add, op1=ALU.add))
            fin(e.scalar_tensor_tensor(pmt[:], qt[:], -128.0, rankt[:],
                                       op0=ALU.mult, op1=ALU.add))
            fin(e.scalar_tensor_tensor(qt[:], eselt[:], 32.0, qt[:],
                                       op0=ALU.mult, op1=ALU.add))
            fin(e.scalar_tensor_tensor(qt[:], pmt[:], 256.0, qt[:],
                                       op0=ALU.mult, op1=ALU.add))
            fin(e.select(sidu[:], masku[:], qt[:], bigt[:]))
        s_r9 = p.step("vector", 11, f_r9, fence=True, plug=True,
                      after=[s_r8, s_init], name="r9")

        def f_r10(e, fin):
            fin(e.dma_start(out=slotv_d, in_=slotu[:]), 16)
            fin(e.dma_start(out=sid_d, in_=sidu[:]), 16)
        s_r10 = p.step("sync", 32, f_r10, after=[s_r9], name="r10")

        fill_s = None
        for fi in range(4):
            def f_fill(e, fin, fi=fi):
                fin(e.dma_start(
                    out=s2tp_d[fi * 8192:(fi + 1) * 8192, :].rearrange(
                        "(j p) o -> p j o", p=128),
                    in_=bigu[:].rearrange("p (j o) -> p j o", o=2),
                    allow_slow_non_contiguous=True), 16)
            fill_s = p.step("gpsimd", 16, f_fill,
                            after=[s_init] + ([fill_s] if fill_s else []),
                            name=f"fill{fi}")

        fencev_d = nc.dram_tensor("fencev_d", [128, 2], U32, kind="Internal").ap()

        def f_myl(e, fin):
            cid = e.partition_id()
            fin(e.dma_start(out=myslot[:],
                            in_=sid_d[:, bass.ts(cid, 32)]), 16)
            cid2 = e.partition_id()
            fin(e.dma_start(out=myslot[:],
                            in_=sid_d[:, bass.ts(cid2, 32)]), 16)
        s_myl = p.step("gpsimd", 32, f_myl, after=[s_r10], name="myl")

        sc_prev = [s_myl, fill_s, s_const]
        for j in range(32):
            def f_scat(e, fin, j=j):
                fin(e.indirect_dma_start(
                    out=s2tp_d[0:128, :],
                    out_offset=IndirectOffsetOnAxis(ap=myslot[:, j:j + 1], axis=0),
                    in_=tokt[:, 2 * j:2 * j + 2], in_offset=None,
                    bounds_check=8 * CAP - 1, oob_is_err=False), 16)
            sc_prev = [p.step("gpsimd", 16, f_scat, after=sc_prev, name=f"sc{j}")]

        def f_scfence(e, fin):
            fin(e.dma_start(out=slotu[:],
                            in_=s2tp_d.rearrange("(p r) o -> p (r o)", p=128)[:, 0:256]), 16)
        s_scf = p.step("gpsimd", 16, f_scfence, after=sc_prev, name="scfence")
        p.step("sync", 0, lambda e, fin: None, after=[s_scf, s_r10], name="done")
        p.emit()
    return nc


def build_LD2(debug=False):
    CAP = 4096
    nc = bass.Bass("TRN2", target_bir_lowering=False)
    parts = nc.dram_tensor("parts", [8, 128, 64], U32, kind="ExternalInput").ap()
    xn1f = nc.dram_tensor("xn1f", [8 * T, D], F32, kind="ExternalInput").ap()
    w1_d = nc.dram_tensor("w1_d", [D, 2 * DI], F32, kind="ExternalInput").ap()
    w2_d = nc.dram_tensor("w2_d", [2 * DI, D], F32, kind="ExternalInput").ap()

    y_d = nc.dram_tensor("y_d", [CAP, D], F32, kind="ExternalOutput").ap()
    xb_d = nc.dram_tensor("xb_d", [CAP, D], BF16,
                          kind="ExternalOutput" if debug else "Internal").ap()
    s2t_dbg = (nc.dram_tensor("s2t_dbg", [128, 64], U32, kind="ExternalOutput").ap()
               if debug else None)
    fence_d = nc.dram_tensor("fence_d", [128, 64], F32, kind="Internal").ap()

    ctx = ExitStack()
    with ctx:
        sb = lambda name, shape, dt=F32: ctx.enter_context(nc.sbuf_tensor(name, shape, dt))
        mrg = [sb(f"mrg{i}", [128, 64], U32) for i in range(8)]
        s2tt = sb("s2tt", [128, 64], U32)
        xg = [sb(f"xg{i}", [128, D]) for i in range(2)]
        xgb = [sb(f"xgb{i}", [128, D], BF16) for i in range(2)]
        w1b = [sb(f"w1b{k}", [128, 2 * DI], BF16) for k in range(8)]
        w2g = [[sb(f"w2g{i}_{j}", [128, D], BF16) for j in range(8)] for i in range(2)]
        xbT = [[sb(f"xbT{i}_{k}", [128, 512], BF16) for k in range(8)] for i in range(2)]
        h1c = [sb(f"h1c{f}", [128, 512], BF16) for f in range(32)]
        ytile = sb("ytile", [128, D])
        ps = [ctx.enter_context(nc.psum_tensor(f"ps{i}", [128, 512], F32))
              for i in range(8)]

        p = Prog(nc, ctx)

        def f_w1l(e, fin):
            for _rep in range(2):
                for k in range(8):
                    fin(e.dma_start(out=w1b[k][:],
                                    in_=w1_d[k * 128:(k + 1) * 128, :]), 16)
        s_w1l = p.step("gpsimd", 256, f_w1l, name="w1l")

        acc = s2tt
        ld_s = {}
        for pt in range(8):
            def f_ml(e, fin, pt=pt):
                fin(e.dma_start(out=mrg[pt][:], in_=parts[pt, :, :]), 16)
                fin(e.dma_start(out=mrg[pt][:], in_=parts[pt, :, :]), 16)
            ld_s[(pt, "ld")] = p.step("sync", 32, f_ml, name=f"ml{pt}")

            def f_mn(e, fin, pt=pt):
                if pt == 0:
                    fin(e.tensor_copy(acc[:], mrg[0][:]))
                else:
                    fin(e.tensor_tensor(acc[:], acc[:], mrg[pt][:],
                                        op=ALU.min))
            ld_s[(pt, "mn")] = p.step("vector", 1, f_mn,
                                      after=[ld_s[(pt, "ld")]], fence=True,
                                      plug=True, name=f"mn{pt}")
        s_merged = ld_s[(7, "mn")]
        if debug:
            def f_s2dbg(e, fin):
                fin(e.dma_start(out=s2t_dbg, in_=s2tt[:]), 16)
            s_merged = p.step("sync", 16, f_s2dbg, after=[s_merged], name="s2dbg")

        g_s, cst_s, xbst_s = {}, {}, {}
        for sc in range(32):
            par = sc % 2
            dep = [s_merged]
            if sc >= 2:
                dep.append(cst_s[sc - 2])

            def f_g(e, fin, sc=sc, par=par):
                fin(e.indirect_dma_start(
                    out=xg[par][:], out_offset=None, in_=xn1f,
                    in_offset=IndirectOffsetOnAxis(ap=s2tt[:, 2 * sc:2 * sc + 1], axis=0),
                    bounds_check=8 * T - 1, oob_is_err=False), 16)
                fin(e.dma_start(out=fence_d, in_=xg[par][:, 0:64]), 16)
            g_s[sc] = p.step("gpsimd", 32, f_g, after=dep, name=f"g{sc}")

            def f_cst(e, fin, par=par):
                fin(e.tensor_copy(xgb[par][:], xg[par][:]))
            cdep = [g_s[sc]]
            if sc >= 2:
                cdep.append(xbst_s[sc - 2])
            cst_s[sc] = p.step("vector", 1, f_cst, after=cdep, plug=True, name=f"cst{sc}")

            def f_xbst(e, fin, sc=sc, par=par):
                fin(e.dma_start(out=xb_d[sc * 128:(sc + 1) * 128, :],
                                in_=xgb[par][:]), 16)
            xbst_s[sc] = p.step("sync", 16, f_xbst, after=[cst_s[sc]],
                                name=f"xbst{sc}")
        s_xb_done = xbst_s[31]

        NT2 = ntiles(D, 512)
        ydma_s, h1_s, tl_s, w2l_s, ymm_s, yev_s = {}, {}, {}, {}, {}, {}
        for sti in range(8):
            par = sti % 2

            def f_tl(e, fin, sti=sti, par=par):
                for _rep in range(2):
                    for k in range(8):
                        fin(e.dma_start_transpose(
                            xbT[par][k][:],
                            xb_d[sti * 512:(sti + 1) * 512,
                                 k * 128:(k + 1) * 128]), 16)
            tdep = [s_xb_done]
            if sti >= 2:
                tdep.append(h1_s[(sti - 2, 31)])
            tl_s[sti] = p.step("scalar", 256, f_tl, after=tdep, name=f"tl{sti}")

            for fc in range(32):
                psn = ps[fc % 2]
                mdep = [tl_s[sti], s_w1l]
                if (sti, fc - 2) in h1_s:
                    mdep.append(h1_s[(sti, fc - 2)])
                elif sti >= 1:
                    mdep.append(yev_s[(sti - 1, 3)])

                def f_h1(e, fin, sti=sti, fc=fc, par=par, psn=psn):
                    for k in range(8):
                        fin(e.matmul(psn[:], w1b[k][:, fc * 128:(fc + 1) * 128],
                                     xbT[par][k][:], start=(k == 0), stop=(k == 7)))
                s_h1m = p.step("tensor", 8, f_h1, after=mdep, name=f"h1m{sti}_{fc}")

                def f_h1e(e, fin, fc=fc, psn=psn):
                    fin(e.activation(out=h1c[fc][:], in_=psn[:],
                                     func=AF.Gelu_apprx_tanh))
                h1_s[(sti, fc)] = p.step("scalar", 1, f_h1e, after=[s_h1m],
                                         name=f"h1e{sti}_{fc}")

            for kg in range(4):
                par2 = kg % 2

                def f_w2l(e, fin, kg=kg, par2=par2):
                    for _rep in range(2):
                        for j in range(8):
                            fin(e.dma_start(out=w2g[par2][j][:],
                                            in_=w2_d[(kg * 8 + j) * 128:(kg * 8 + j + 1) * 128, :]), 16)
                wdep = []
                if (sti, kg - 2, 3) in ymm_s:
                    wdep.append(ymm_s[(sti, kg - 2, 3)])
                elif sti >= 1 and kg < 2:
                    wdep.append(ymm_s[(sti - 1, kg + 2, 3)])
                w2l_s[(sti, kg)] = p.step("gpsimd", 256, f_w2l, after=wdep,
                                          name=f"w2l{sti}_{kg}")

                for sm in range(4):
                    def f_ymm(e, fin, sti=sti, kg=kg, sm=sm, par2=par2):
                        for nt, (no, nw) in enumerate(NT2):
                            for j in range(8):
                                fin(e.matmul(ps[sm * 2 + nt][:, 0:nw],
                                             h1c[kg * 8 + j][:, sm * 128:(sm + 1) * 128],
                                             w2g[par2][j][:, no:no + nw],
                                             start=(kg == 0 and j == 0),
                                             stop=(kg == 3 and j == 7)))
                    ydep = [w2l_s[(sti, kg)], h1_s[(sti, 31)]]
                    if kg == 0:
                        if sti >= 1:
                            ydep.append(ydma_s[(sti - 1, sm)])
                    else:
                        ydep.append(ymm_s[(sti, kg - 1, sm)])
                    ymm_s[(sti, kg, sm)] = p.step("tensor", 16, f_ymm, after=ydep,
                                                  name=f"ymm{sti}_{kg}_{sm}")

            for sm in range(4):
                def f_yev(e, fin, sm=sm):
                    for nt, (no, nw) in enumerate(NT2):
                        fin(e.copy(ytile[:, no:no + nw], ps[sm * 2 + nt][:, 0:nw]))
                ydepe = [ymm_s[(sti, 3, sm)]]
                if sm >= 1:
                    ydepe.append(ydma_s[(sti, sm - 1)])
                s_yev = p.step("scalar", 2, f_yev, after=ydepe, name=f"yev{sti}_{sm}")
                yev_s[(sti, sm)] = s_yev

                def f_ydma(e, fin, sti=sti, sm=sm):
                    fin(e.dma_start(out=y_d[sti * 512 + sm * 128:
                                            sti * 512 + (sm + 1) * 128, :],
                                    in_=ytile[:]), 16)
                ydma_s[(sti, sm)] = p.step("sync", 16, f_ydma, after=[s_yev],
                                           name=f"ydma{sti}_{sm}")

        p.step("sync", 0, lambda e, fin: None, after=[ydma_s[(7, 3)]], name="done")
        p.emit()
    return nc


def build_LE():
    nc = bass.Bass("TRN2", target_bir_lowering=False)
    x1T = nc.dram_tensor("x1T", [D, T], F32, kind="ExternalInput").ap()
    yall = nc.dram_tensor("yall", [8 * 4096 + 1, D], F32, kind="ExternalInput").ap()
    slot1 = nc.dram_tensor("slot1", [128, 16], U32, kind="ExternalInput").ap()
    slot2 = nc.dram_tensor("slot2", [128, 16], U32, kind="ExternalInput").ap()
    g1c = nc.dram_tensor("g1c", [128, 16], F32, kind="ExternalInput").ap()
    g2c = nc.dram_tensor("g2c", [128, 16], F32, kind="ExternalInput").ap()
    biasr = nc.dram_tensor("biasr", [128, D], F32, kind="ExternalInput").ap()
    eye128 = nc.dram_tensor("eye128", [128, 128], F32, kind="ExternalInput").ap()

    x2_d = nc.dram_tensor("x2_d", [D, T], F32, kind="ExternalOutput").ap()

    ctx = ExitStack()
    with ctx:
        sb = lambda name, shape, dt=F32: ctx.enter_context(nc.sbuf_tensor(name, shape, dt))
        s1t = sb("s1t", [128, 16], U32)
        s2t = sb("s2t", [128, 16], U32)
        g1t = sb("g1t", [128, 16])
        g2t = sb("g2t", [128, 16])
        bias = sb("bias", [128, D])
        eye = sb("eye", [128, 128])
        ga = [sb(f"ga{i}", [128, D]) for i in range(2)]
        gb = [sb(f"gb{i}", [128, D]) for i in range(2)]
        moe = [sb(f"moe{i}", [128, D]) for i in range(16)]
        x2c = sb("x2c", [128, T])
        fence_d = nc.dram_tensor("fence_e", [128, 64], F32, kind="Internal").ap()
        psT = ctx.enter_context(nc.psum_tensor("psT", [128, 128], F32))

        p = Prog(nc, ctx)

        def f_const(e, fin):
            for _rep in range(2):
                fin(e.dma_start(out=s1t[:], in_=slot1), 16)
                fin(e.dma_start(out=s2t[:], in_=slot2), 16)
                fin(e.dma_start(out=g1t[:], in_=g1c), 16)
                fin(e.dma_start(out=g2t[:], in_=g2c), 16)
                fin(e.dma_start(out=bias[:], in_=biasr), 16)
                fin(e.dma_start(out=eye[:], in_=eye128), 16)
        s_const = p.step("sync", 192, f_const, name="const")

        gth_s, cmb_s = {}, {}
        for tc in range(16):
            par = tc % 2
            gdep = [s_const]
            if tc >= 2:
                gdep.append(cmb_s[tc - 2])

            def f_g(e, fin, tc=tc, par=par):
                fin(e.memset(ga[par][:], 0.0), 1)
                fin(e.memset(gb[par][:], 0.0), 1)
            s_z = p.step("vector", 2, f_g, after=gdep, name=f"z{tc}")

            def f_gth(e, fin, tc=tc, par=par):
                fin(e.indirect_dma_start(
                    out=ga[par][:], out_offset=None, in_=yall,
                    in_offset=IndirectOffsetOnAxis(ap=s1t[:, tc:tc + 1], axis=0),
                    bounds_check=8 * 4096 - 1, oob_is_err=False), 16)
                fin(e.dma_start(out=fence_d, in_=ga[par][:, 0:64]), 16)
                fin(e.indirect_dma_start(
                    out=gb[par][:], out_offset=None, in_=yall,
                    in_offset=IndirectOffsetOnAxis(ap=s2t[:, tc:tc + 1], axis=0),
                    bounds_check=8 * 4096 - 1, oob_is_err=False), 16)
                fin(e.dma_start(out=fence_d, in_=gb[par][:, 0:64]), 16)
            gth_s[tc] = p.step("gpsimd", 64, f_gth, after=[s_z], name=f"gth{tc}")

            def f_cmb(e, fin, tc=tc, par=par):
                fin(e.tensor_scalar(moe[tc][:], ga[par][:], g1t[:, tc:tc + 1],
                                    None, op0=ALU.mult), 1)
                fin(e.scalar_tensor_tensor(moe[tc][:], gb[par][:],
                                           g2t[:, tc:tc + 1], moe[tc][:],
                                           op0=ALU.mult, op1=ALU.add), 1)
                fin(e.tensor_add(moe[tc][:], moe[tc][:], bias[:]), 1)
            cmb_s[tc] = p.step("vector", 3, f_cmb, fence=True, plug=True,
                               after=[gth_s[tc], s_const], name=f"cmb{tc}")

        ld_s, add_s, odma_s = {}, {}, {}
        for m in range(8):
            def f_ldx1(e, fin, m=m):
                fin(e.dma_start(out=x2c[:], in_=x1T[m * 128:(m + 1) * 128, :]), 16)
            ld_s[m] = p.step("sync", 16, f_ldx1,
                             after=[odma_s[m - 1]] if m >= 1 else [],
                             name=f"ldx1_{m}")
            prev_add = None
            for tc in range(16):
                par = tc % 2

                def f_tp(e, fin, m=m, tc=tc):
                    fin(e.transpose(psT[:], moe[tc][:, m * 128:(m + 1) * 128],
                                    eye[:]), 1)
                tdep = [cmb_s[tc], s_const]
                if prev_add is not None:
                    tdep.append(prev_add)
                elif m >= 1:
                    tdep.append(add_s[(m - 1, 15)])
                s_tp = p.step("tensor", 1, f_tp, after=tdep, name=f"tp{m}_{tc}")

                def f_add(e, fin, m=m, tc=tc):
                    fin(e.tensor_add(x2c[:, tc * 128:(tc + 1) * 128],
                                     x2c[:, tc * 128:(tc + 1) * 128], psT[:]), 1)
                prev_add = p.step("vector", 1, f_add,
                                  after=[s_tp, ld_s[m]], name=f"add{m}_{tc}")
                add_s[(m, tc)] = prev_add

            def f_odma(e, fin, m=m):
                fin(e.dma_start(out=x2_d[m * 128:(m + 1) * 128, :], in_=x2c[:]), 16)
            odma_s[m] = p.step("sync", 16, f_odma, after=[add_s[(m, 15)]],
                               name=f"odma{m}")

        p.step("sync", 0, lambda e, fin: None, after=[odma_s[7]], name="done")
        p.emit()
    return nc


def _ln_stats_bcast(p, nc, tag, src_dram, NT4, tiles, deps):
    x0sub, sqsub = tiles["x0sub"], tiles["sqsub"]
    sumx, sumx2, rtmp, rstd = tiles["sumx"], tiles["sumx2"], tiles["rtmp"], tiles["rstd"]
    onc, on1, bc_r, bc_n = tiles["onc"], tiles["on1"], tiles["bc_r"], tiles["bc_n"]
    psA, psB = tiles["psA"], tiles["psB"]
    nmr = rtmp
    ld_s, sq_s, mm_s, ev_s = {}, {}, {}, {}
    for ni, (no, nw) in enumerate(NT4):
        for k in range(8):
            par = k % 2
            it = (ni, k)
            prev = list(deps)
            key2 = (ni, k - 2) if k >= 2 else ((ni - 1, 6 + k) if ni else None)
            if key2 in mm_s:
                prev.append(mm_s[key2])

            def f_ld(e, fin, k=k, no=no, nw=nw, par=par):
                fin(e.dma_start(out=x0sub[par][:, 0:nw],
                                in_=src_dram[k * 128:(k + 1) * 128, no:no + nw]), 16)
            ld_s[it] = p.step("sync", 16, f_ld, after=prev, name=f"{tag}ld{it}")

            def f_sq(e, fin, nw=nw, par=par):
                fin(e.activation(out=sqsub[par][:, 0:nw],
                                 in_=x0sub[par][:, 0:nw], func=AF.Square), 1)
            sq_s[it] = p.step("scalar", 1, f_sq, after=[ld_s[it]] + prev,
                              plug=True, name=f"{tag}sq{it}")

            def f_mm(e, fin, k=k, nw=nw, par=par):
                fin(e.matmul(psA[0:1, 0:nw], onc[:], x0sub[par][:, 0:nw],
                             start=(k == 0), stop=(k == 7)), 1)
                fin(e.matmul(psB[0:1, 0:nw], onc[:], sqsub[par][:, 0:nw],
                             start=(k == 0), stop=(k == 7)), 1)
            mm_s[it] = p.step(
                "tensor", 2, f_mm,
                after=[ld_s[it], sq_s[it]] +
                      ([ev_s[ni - 1]] if (k == 0 and ni > 0) else []),
                name=f"{tag}mm{it}")

        def f_ev(e, fin, no=no, nw=nw):
            fin(e.copy(sumx[0:1, no:no + nw], psA[0:1, 0:nw]), 1)
            fin(e.copy(sumx2[0:1, no:no + nw], psB[0:1, 0:nw]), 1)
        ev_s[ni] = p.step("scalar", 2, f_ev, after=[mm_s[(ni, 7)]],
                          name=f"{tag}ev{ni}")

    def f_rm(e, fin):
        fin(e.tensor_scalar_mul(sumx[:], sumx[:], 1.0 / D), 1)
        fin(e.tensor_scalar_mul(sumx2[:], sumx2[:], 1.0 / D), 1)
        fin(e.tensor_mul(rtmp[:], sumx[:], sumx[:]), 1)
        fin(e.tensor_sub(sumx2[:], sumx2[:], rtmp[:]), 1)
        fin(e.tensor_scalar_add(sumx2[:], sumx2[:], LN_EPS), 1)
    s_rm = p.step("vector", 5, f_rm, after=[ev_s[len(NT4) - 1]],
                  name=f"{tag}rm", fence=True)

    def f_rs(e, fin):
        fin(e.activation(out=rtmp[:], in_=sumx2[:], func=AF.Ln), 1)
        fin(e.activation(out=rstd[:], in_=rtmp[:], func=AF.Exp, scale=-0.5), 1)
    s_rs = p.step("scalar", 2, f_rs, after=[s_rm], name=f"{tag}rs", fence=True)

    def f_nm(e, fin):
        fin(e.tensor_mul(nmr[:], sumx[:], rstd[:]), 1)
        fin(e.tensor_scalar_mul(nmr[:], nmr[:], -1.0), 1)
    s_nm = p.step("vector", 2, f_nm, after=[s_rs], name=f"{tag}nm", fence=True)

    bc_last = None
    for ni, (no, nw) in enumerate(NT4):
        def f_bc(e, fin, no=no, nw=nw):
            fin(e.matmul(psA[:, 0:nw], on1[:], rstd[0:1, no:no + nw],
                         start=True, stop=True), 1)
            fin(e.matmul(psB[:, 0:nw], on1[:], nmr[0:1, no:no + nw],
                         start=True, stop=True), 1)
        s_bc = p.step("tensor", 2, f_bc,
                      after=[s_nm] + ([bc_last] if bc_last else []),
                      name=f"{tag}bc{ni}")

        def f_be(e, fin, no=no, nw=nw):
            fin(e.copy(bc_r[:, no:no + nw], psA[:, 0:nw]), 1)
            fin(e.copy(bc_n[:, no:no + nw], psB[:, 0:nw]), 1)
        bc_last = p.step("scalar", 2, f_be, after=[s_bc], name=f"{tag}be{ni}")
    return bc_last


def build_LF():
    TE = T + 3
    nc = bass.Bass("TRN2", target_bir_lowering=False)
    x2e = nc.dram_tensor("x2e", [D, TE], F32, kind="ExternalInput").ap()
    winT = nc.dram_tensor("winT", [D, 2 * DI], F32, kind="ExternalInput").ap()
    wxpT = nc.dram_tensor("wxpT", [128, 16 * NST], F32, kind="ExternalInput").ap()
    wdt_d = nc.dram_tensor("wdt_d", [DI, DI], F32, kind="ExternalInput").ap()
    bdt = nc.dram_tensor("bdt", [128, 16], F32, kind="ExternalInput").ap()
    alog = nc.dram_tensor("alog", [128, 16 * NST], F32, kind="ExternalInput").ap()
    convw = nc.dram_tensor("convw", [128, 64], F32, kind="ExternalInput").ap()
    convb = nc.dram_tensor("convb", [128, 16], F32, kind="ExternalInput").ap()
    onesc = nc.dram_tensor("onesc", [128, 1], F32, kind="ExternalInput").ap()
    ones1 = nc.dram_tensor("ones1", [1, 128], F32, kind="ExternalInput").ap()

    xc_d = nc.dram_tensor("xc_d", [DI, T], F32, kind="ExternalOutput").ap()
    z_d = nc.dram_tensor("z_d", [DI, T], F32, kind="ExternalOutput").ap()
    delta_d = nc.dram_tensor("delta_d", [DI, T], F32, kind="ExternalOutput").ap()
    bt_d = nc.dram_tensor("bt_d", [NST, T], F32, kind="ExternalOutput").ap()
    hend_d = nc.dram_tensor("hend_d", [128, 16 * NST], F32, kind="ExternalOutput").ap()

    NT = ntiles(TE, 512)
    NT_T = ntiles(T, 512)

    ctx = ExitStack()
    with ctx:
        sb = lambda name, shape, dt=F32: ctx.enter_context(nc.sbuf_tensor(name, shape, dt))
        wst = [[sb(f"wst{i}_{k}", [128, 128], BF16) for k in range(8)]
               for i in range(2)]
        wxt = sb("wxt", [128, 16 * NST], BF16)
        wdtm = [sb(f"wdtm{k}", [128, 128], BF16) for k in range(16)]
        cw = sb("cw", [128, 64])
        cb = sb("cb", [128, 16])
        bdtt = sb("bdtt", [128, 16])
        nega = sb("nega", [128, 16 * NST])
        onc = sb("onc", [128, 1])
        on1 = sb("on1", [1, 128])
        bc_r = sb("bc_r", [128, TE])
        bc_n = sb("bc_n", [128, TE])
        xtmp = sb("xtmp", [128, TE])
        deltaT = sb("deltaT", [128, TE])
        xccT = sb("xccT", [128, TE])
        dscT = sb("dscT", [128, TE])
        dxT = sb("dxT", [128, TE])
        xz = [sb("xz0", [128, TE])] * 2
        xcs = sb("xcs", [128, T])
        xn_bf = [sb(f"xnbf{k}", [128, TE], BF16) for k in range(8)]
        xcsb = [sb(f"xcsb{i}", [128, T], BF16) for i in range(16)]
        btrow = [sb("btrow0", [1, T])] * 2
        hend = sb("hend", [128, 16 * NST])
        sumx = deltaT[0:1, 0:TE] if False else None
        conva = xtmp
        delta = [deltaT] * 2
        xcc = [xccT] * 2
        dsc = dscT
        dx = dxT
        aa = bc_r
        h = bc_n
        dpre = xtmp
        x0c = dscT
        x0sub = [xcs, xcs]
        sqsub = [xcs, xcs]
        psA = ctx.enter_context(nc.psum_tensor("psA", [128, 512], F32))
        psB = ctx.enter_context(nc.psum_tensor("psB", [128, 512], F32))
        psC = ctx.enter_context(nc.psum_tensor("psC", [NST, T], F32))

        p = Prog(nc, ctx)

        def f_wload(e, fin):
            fin(e.dma_start(out=wxt[:], in_=wxpT), 16)
        s_wload = p.step("gpsimd", 16, f_wload, name="wload")

        def f_cload(e, fin):
            for _rep in range(2):
                fin(e.dma_start(out=cw[:], in_=convw), 16)
                fin(e.dma_start(out=cb[:], in_=convb), 16)
                fin(e.dma_start(out=bdtt[:], in_=bdt), 16)
                fin(e.dma_start(out=nega[:], in_=alog), 16)
                fin(e.dma_start(out=onc[:], in_=onesc), 16)
                fin(e.dma_start(out=on1[:], in_=ones1), 16)
        s_cload = p.step("sync", 192, f_cload, name="cload")

        def f_nega(e, fin):
            fin(e.activation(out=nega[:], in_=nega[:], func=AF.Exp), 1)
        s_negae = p.step("scalar", 1, f_nega, after=[s_cload], name="negaexp")

        def f_negam(e, fin):
            fin(e.tensor_scalar_mul(nega[:], nega[:], -1.0), 1)
        s_nega = p.step("vector", 1, f_negam, after=[s_negae], fence=True,
                        plug=True, name="negamul")

        tiles = dict(
            x0sub=[_SubView(xcs, 0), _SubView(xcs, 512)],
            sqsub=[_SubView(xcs, 1024), _SubView(xcs, 1536)],
            sumx=_RowView(deltaT), sumx2=_RowView(xccT),
            rtmp=_RowView(dscT), rstd=_RowView(dxT),
            onc=onc, on1=on1, bc_r=bc_r, bc_n=bc_n, psA=psA, psB=psB)
        s_bc1 = _ln_stats_bcast(p, nc, "s1", x2e, NT, tiles, [s_cload])

        xn_steps, ldb_steps = {}, {}
        for k in range(8):
            prev = [xn_steps[k - 1]] if k >= 1 else [s_bc1]

            def f_ldb(e, fin, k=k):
                fin(e.dma_start(out=x0c[:], in_=x2e[k * 128:(k + 1) * 128, :]), 16)
            ldb_steps[k] = p.step("sync", 16, f_ldb, after=prev, name=f"ldB{k}")

            def f_xn(e, fin, k=k):
                fin(e.tensor_mul(xtmp[:], x0c[:], bc_r[:]), 1)
                fin(e.tensor_add(xn_bf[k][:], xtmp[:], bc_n[:]), 1)
            xn_steps[k] = p.step("vector", 2, f_xn, plug=True,
                                 after=[ldb_steps[k], s_bc1], name=f"xn{k}")
        s_xn_last = xn_steps[7]

        wl_s = {}
        mmx, evx = {}, {}
        conv_s, silu_s, cast_s, mmb_s, xdma_s, zdma_s = {}, {}, {}, {}, {}, {}
        for m in range(32):
            par = m % 2
            ps_m = psA if par == 0 else psB
            inner = m < 16

            def f_wl(e, fin, m=m):
                for _rep in range(2):
                    for k in range(8):
                        fin(e.dma_start(out=wst[m % 2][k][:],
                                        in_=winT[k * 128:(k + 1) * 128,
                                                 m * 128:(m + 1) * 128]), 16)
            wl_s[m] = p.step("gpsimd", 256, f_wl,
                             after=[mmx[(m - 2, len(NT) - 1)]] if m >= 2 else [],
                             name=f"wl{m}")
            for ni, (no, nw) in enumerate(NT):
                deps = [s_xn_last, wl_s[m]]
                if ni > 0:
                    deps.append(evx[(m, ni - 1)])
                elif m >= 2:
                    deps.append(evx[(m - 2, len(NT) - 1)])
                if m < 2:
                    deps.append(s_bc1)

                def f_mmx(e, fin, m=m, no=no, nw=nw, ps_m=ps_m):
                    for k in range(8):
                        fin(e.matmul(ps_m[:, 0:nw],
                                     wst[m % 2][k][:],
                                     xn_bf[k][:, no:no + nw],
                                     start=(k == 0), stop=(k == 7)), 1)
                mmx[(m, ni)] = p.step("tensor", 8, f_mmx, after=deps,
                                      name=f"mmx{m}_{ni}")

                evd = [mmx[(m, ni)]]
                if m >= 1:
                    evd.append(conv_s[m - 1] if (m - 1) < 16 else zdma_s[m - 1])

                def f_evx(e, fin, no=no, nw=nw, par=par, ps_m=ps_m):
                    fin(e.copy(xz[par][:, no:no + nw], ps_m[:, 0:nw]), 1)
                evx[(m, ni)] = p.step("scalar", 1, f_evx, after=evd,
                                      name=f"evx{m}_{ni}")
            if inner:
                cdeps = [evx[(m, len(NT) - 1)], s_cload]
                if m >= 1:
                    cdeps.append(silu_s[m - 1])

                def f_conv(e, fin, m=m, par=par):
                    fin(e.tensor_scalar(conva[:, 0:T], xz[par][:, 0:T],
                                        cw[:, m * 4:m * 4 + 1], cb[:, m:m + 1],
                                        op0=ALU.mult, op1=ALU.add), 1)
                    for j in range(1, 4):
                        fin(e.scalar_tensor_tensor(conva[:, 0:T], xz[par][:, j:j + T],
                                                   cw[:, m * 4 + j:m * 4 + j + 1],
                                                   conva[:, 0:T], op0=ALU.mult,
                                                   op1=ALU.add), 1)
                conv_s[m] = p.step("vector", 4, f_conv, after=cdeps, plug=True,
                                   name=f"conv{m}")

                sdeps = [conv_s[m]]
                if m >= 1:
                    sdeps += [cast_s[m - 1], xdma_s[m - 1]]

                def f_silu(e, fin):
                    fin(e.activation(out=xcs[:], in_=conva[:, 0:T], func=AF.Silu), 1)
                silu_s[m] = p.step("scalar", 1, f_silu, after=sdeps, plug=True,
                                   name=f"silu{m}")

                def f_cast(e, fin, m=m):
                    fin(e.tensor_copy(xcsb[m][:], xcs[:]), 1)
                cast_s[m] = p.step("vector", 1, f_cast, after=[silu_s[m]],
                                   plug=True, name=f"cast{m}")

                def f_mmb(e, fin, m=m):
                    for ni2, (no2, nw2) in enumerate(NT_T):
                        fin(e.matmul(psC[:, no2:no2 + nw2],
                                     wxt[:, m * NST:(m + 1) * NST],
                                     xcsb[m][:, no2:no2 + nw2],
                                     start=(m == 0), stop=(m == 15)), 1)
                mmb_s[m] = p.step("tensor", 4, f_mmb, after=[cast_s[m]],
                                  name=f"mmb{m}")

                def f_xdma(e, fin, m=m):
                    fin(e.dma_start(out=xc_d[m * 128:(m + 1) * 128, :],
                                    in_=xcs[:]), 16)
                xdma_s[m] = p.step("sync", 16, f_xdma, after=[silu_s[m]],
                                   name=f"xdma{m}")
            else:
                def f_zdma(e, fin, m=m, par=par):
                    fin(e.dma_start(out=z_d[(m - 16) * 128:(m - 15) * 128, :],
                                    in_=xz[par][:, 3:3 + T]), 16)
                zdma_s[m] = p.step("sync", 16, f_zdma,
                                   after=[evx[(m, len(NT) - 1)]],
                                   name=f"zdma{m}")

        s_bt = None
        btw = sb("btw", [NST, 512])
        for ni2, (no2, nw2) in enumerate(NT_T):
            def f_evb(e, fin, no2=no2, nw2=nw2):
                fin(e.copy(btw[:, 0:nw2], psC[:, no2:no2 + nw2]), 1)
            s_evb = p.step("scalar", 1, f_evb,
                           after=[mmb_s[15]] + ([s_bt] if s_bt else []),
                           name=f"evb{ni2}")

            def f_btd(e, fin, no2=no2, nw2=nw2):
                fin(e.dma_start(out=bt_d[:, no2:no2 + nw2], in_=btw[:, 0:nw2]), 16)
            s_bt = p.step("sync", 16, f_btd, after=[s_evb], name=f"btd{ni2}")

        ddma_s, dstep_s = {}, {}
        wdl_s = {}
        for m2 in range(16):
            par = m2 % 2

            def f_wdl(e, fin, m2=m2):
                for _rep in range(2):
                    for k in range(16):
                        fin(e.dma_start(out=wdtm[k][:],
                                        in_=wdt_d[k * 128:(k + 1) * 128,
                                                  m2 * 128:(m2 + 1) * 128]), 16)
            wdl_s[m2] = p.step("gpsimd", 2 * 16 * 16, f_wdl,
                               after=[dstep_s[m2 - 1]] if m2 >= 1 else [],
                               name=f"wdl{m2}")

            for ni2, (no2, nw2) in enumerate(NT_T):
                ps_m = psA if ni2 % 2 == 0 else psB
                ddep = [wdl_s[m2], cast_s[15]]
                if (m2, ni2 - 2) in dstep_s:
                    pass
                if ni2 >= 2:
                    ddep.append(dstep_s[(m2, ni2 - 2)])
                elif m2 >= 1:
                    ddep.append(dstep_s[(m2 - 1, ni2 + 2)])
                else:
                    ddep.append(evx[(31, len(NT) - 1)])

                def f_dmm(e, fin, m2=m2, no2=no2, nw2=nw2, ps_m=ps_m):
                    for k in range(16):
                        fin(e.matmul(ps_m[:, 0:nw2], wdtm[k][:],
                                     xcsb[k][:, no2:no2 + nw2],
                                     start=(k == 0), stop=(k == 15)), 1)
                s_dmm = p.step("tensor", 16, f_dmm, after=ddep,
                               name=f"dmm{m2}_{ni2}")

                def f_dev(e, fin, no2=no2, nw2=nw2, ps_m=ps_m):
                    fin(e.copy(dpre[:, no2:no2 + nw2], ps_m[:, 0:nw2]), 1)
                dstep_s[(m2, ni2)] = p.step("scalar", 1, f_dev, after=[s_dmm],
                                            name=f"dev{m2}_{ni2}")
            sdep2 = [dstep_s[(m2, len(NT_T) - 1)], s_cload]
            if m2 >= 2:
                sdep2.append(ddma_s[m2 - 2])

            def f_sp1(e, fin, m2=m2, par=par):
                fin(e.tensor_scalar(dpre[:, 0:T], dpre[:, 0:T],
                                    bdtt[:, m2:m2 + 1], None, op0=ALU.add), 1)
            s_sp1 = p.step("vector", 1, f_sp1, after=sdep2, fence=True,
                           name=f"sp1_{m2}")

            def f_sp2(e, fin, par=par):
                fin(e.activation(out=dsc[:, 0:T], in_=dpre[:, 0:T], func=AF.Abs), 1)
                fin(e.activation(out=dsc[:, 0:T], in_=dsc[:, 0:T], func=AF.Exp,
                                 scale=-1.0), 1)
            s_sp2 = p.step("scalar", 2, f_sp2, after=[s_sp1], fence=True,
                           name=f"sp2_{m2}")

            def f_sp3(e, fin, par=par):
                fin(e.tensor_scalar_add(dsc[:, 0:T], dsc[:, 0:T], 1.0), 1)
            s_sp3 = p.step("vector", 1, f_sp3, after=[s_sp2], fence=True,
                           name=f"sp3_{m2}")

            def f_sp4(e, fin, par=par):
                fin(e.activation(out=dsc[:, 0:T], in_=dsc[:, 0:T], func=AF.Ln), 1)
            s_sp4 = p.step("scalar", 1, f_sp4, after=[s_sp3], fence=True,
                           name=f"sp4_{m2}")

            def f_sp5(e, fin, m2=m2, par=par):
                fin(e.tensor_scalar_max(dpre[:, 0:T], dpre[:, 0:T], 0.0), 1)
                fin(e.tensor_add(delta[par][:, 0:T], dpre[:, 0:T],
                                 dsc[:, 0:T]), 1)
            s_sp5 = p.step("vector", 2, f_sp5, after=[s_sp4], fence=True,
                           plug=True, name=f"sp5_{m2}")

            def f_ddma(e, fin, m2=m2, par=par):
                fin(e.dma_start(out=delta_d[m2 * 128:(m2 + 1) * 128, :],
                                in_=delta[par][:, 0:T]), 16)
            ddma_s[m2] = p.step("sync", 16, f_ddma, after=[s_sp5],
                                name=f"ddma{m2}")
            dstep_s[m2] = ddma_s[m2]

        scan_steps = {}
        hcol_s = {}
        ldc2_s, dxs_s = {}, {}
        ldbt_s, bca_s, u_s, aa_s = {}, {}, {}, {}
        for c in range(16):
            par = c % 2
            cdep = [xdma_s[c], ddma_s[c]]
            if c >= 1:
                cdep += [dxs_s[c - 1], aa_s[(c - 1, NST - 1)]]
            if c == 0:
                cdep.append(ddma_s[15])

            def f_ldc2(e, fin, c=c, par=par):
                for _rep in range(2):
                    fin(e.dma_start(out=xcc[par][:, 0:T],
                                    in_=xc_d[c * 128:(c + 1) * 128, :]), 16)
                    fin(e.dma_start(out=delta[par][:, 0:T],
                                    in_=delta_d[c * 128:(c + 1) * 128, :]), 16)
            ldc2_s[c] = p.step("sync", 64, f_ldc2, after=cdep, name=f"ldc2_{c}")

            ddx = [ldc2_s[c]]
            if c >= 1:
                ddx.append(u_s[(c - 1, NST - 1, 3)])

            def f_dx(e, fin, par=par):
                fin(e.tensor_mul(dx[:, 0:T], delta[par][:, 0:T],
                                 xcc[par][:, 0:T]), 1)
            dxs_s[c] = p.step("vector", 1, f_dx, after=ddx, plug=True,
                              name=f"dx{c}")

            for n in range(NST):
                bdep = [s_bt]
                if n >= 1:
                    bdep.append(u_s[(c, n - 1, 3)])
                elif c >= 1:
                    bdep.append(u_s[(c - 1, NST - 1, 3)])

                def f_ldbt(e, fin, n=n):
                    for _rep in range(2):
                        fin(e.dma_start(out=btrow[0][:], in_=bt_d[n:n + 1, :]), 16)
                ldbt_s[(c, n)] = p.step("scalar", 32, f_ldbt, after=bdep,
                                        name=f"ldbt{c}_{n}")

                for nt2, (no2, nw2) in enumerate(NT_T):
                    ps_n2 = psA if nt2 % 2 == 0 else psB

                    def f_bca(e, fin, no2=no2, nw2=nw2, ps_n2=ps_n2):
                        fin(e.matmul(ps_n2[:, 0:nw2], on1[:],
                                     btrow[0][0:1, no2:no2 + nw2],
                                     start=True, stop=True), 1)
                    bdep2 = [ldbt_s[(c, n)]]
                    if (c, n, nt2 - 2) in u_s:
                        bdep2.append(u_s[(c, n, nt2 - 2)])
                    elif n >= 1:
                        bdep2.append(u_s[(c, n - 1, nt2 + 2)])
                    elif c >= 1:
                        bdep2.append(u_s[(c - 1, NST - 1, nt2 + 2)])
                    elif nt2 < 2:
                        bdep2.append(dstep_s[15])
                    bca_s[(c, n, nt2)] = p.step("tensor", 1, f_bca, after=bdep2,
                                                name=f"bca{c}_{n}_{nt2}")

                    def f_u(e, fin, no2=no2, nw2=nw2, ps_n2=ps_n2):
                        fin(e.tensor_mul(aa[:, no2:no2 + nw2], dx[:, no2:no2 + nw2],
                                         ps_n2[:, 0:nw2]), 1)
                    u_s[(c, n, nt2)] = p.step(
                        "vector", 1, f_u, plug=True,
                        after=[bca_s[(c, n, nt2)], dxs_s[c]] +
                              ([scan_steps[(c, n - 1)]] if (nt2 == 0 and n >= 1) else
                               ([scan_steps[(c - 1, NST - 1)]] if (nt2 == 0 and c >= 1) else [])),
                        name=f"u{c}_{n}_{nt2}")

                def f_aa(e, fin, c=c, n=n, par=par):
                    fin(e.activation(out=dsc[:, 0:T], in_=delta[par][:, 0:T],
                                     func=AF.Exp,
                                     scale=nega[:, c * NST + n:c * NST + n + 1]), 1)
                aa_s[(c, n)] = p.step("scalar", 1, f_aa,
                                      after=[ldc2_s[c], s_nega] +
                                            ([scan_steps[(c, n - 1)]] if n >= 1 else
                                             ([scan_steps[(c - 1, NST - 1)]] if c >= 1 else [])),
                                      plug=True, name=f"aa{c}_{n}")

                def f_scan(e, fin, c=c, n=n):
                    fin(e.tensor_tensor_scan(h[:, 0:T], dsc[:, 0:T], aa[:, 0:T],
                                             0.0, op0=ALU.mult, op1=ALU.add), 1)
                scan_steps[(c, n)] = p.step(
                    "vector", 1, f_scan, plug=True,
                    after=[u_s[(c, n, 3)], aa_s[(c, n)]] +
                          ([hcol_s[(c, n - 1)]] if n >= 1 else
                           ([hcol_s[(c - 1, NST - 1)]] if c >= 1 else [])),
                    name=f"scan{c}_{n}")

                def f_hcol(e, fin, c=c, n=n):
                    fin(e.copy(hend[:, c * NST + n:c * NST + n + 1],
                               h[:, T - 1:T]), 1)
                hcol_s[(c, n)] = p.step("scalar", 1, f_hcol,
                                        after=[scan_steps[(c, n)]], plug=True,
                                        name=f"hcol{c}_{n}")

        def f_hdma(e, fin):
            fin(e.dma_start(out=hend_d, in_=hend[:]), 16)
        s_hdma = p.step("sync", 16, f_hdma, after=[hcol_s[(15, 15)]],
                        name="hdma")
        p.step("sync", 0, lambda e, fin: None, after=[s_hdma], name="done")
        p.emit()
    return nc


def build_LG():
    nc = bass.Bass("TRN2", target_bir_lowering=False)
    xc_d = nc.dram_tensor("xc_d", [DI, T], F32, kind="ExternalInput").ap()
    z_d = nc.dram_tensor("z_d", [DI, T], F32, kind="ExternalInput").ap()
    delta_d = nc.dram_tensor("delta_d", [DI, T], F32, kind="ExternalInput").ap()
    bt_d = nc.dram_tensor("bt_d", [NST, T], F32, kind="ExternalInput").ap()
    carr_d = nc.dram_tensor("carr_d", [128, 16 * NST], F32, kind="ExternalInput").ap()
    alog = nc.dram_tensor("alog", [128, 16 * NST], F32, kind="ExternalInput").ap()
    dpc = nc.dram_tensor("dpc", [128, 16], F32, kind="ExternalInput").ap()
    x2T = nc.dram_tensor("x2T", [D, T], F32, kind="ExternalInput").ap()
    wout_d = nc.dram_tensor("wout_d", [DI, D], F32, kind="ExternalInput").ap()
    onesc = nc.dram_tensor("onesc", [128, 1], F32, kind="ExternalInput").ap()
    ones1 = nc.dram_tensor("ones1", [1, 128], F32, kind="ExternalInput").ap()

    x3_d = nc.dram_tensor("x3_d", [D, T], F32, kind="ExternalOutput").ap()
    NT_T = ntiles(T, 512)

    ctx = ExitStack()
    with ctx:
        sb = lambda name, shape, dt=F32: ctx.enter_context(nc.sbuf_tensor(name, shape, dt))
        ybf = [sb(f"ybf{c}", [128, T], BF16) for c in range(16)]
        carr = sb("carr", [128, 16 * NST])
        nega = sb("nega", [128, 16 * NST])
        dpt = sb("dpt", [128, 16])
        onc = sb("onc", [128, 1])
        on1 = sb("on1", [1, 128])
        xcc = sb("xcc", [128, T])
        delta = sb("delta", [128, T])
        dx = sb("dx", [128, T])
        aa = sb("aa", [128, T])
        dsc = sb("dsc", [128, T])
        h = sb("h", [128, T])
        yacc = sb("yacc", [128, T])
        ytmp = sb("ytmp", [128, 512])
        zt = sb("zt", [128, T])
        btrow = sb("btrow", [1, T])
        wpm = [sb(f"wpm{k}", [128, 128], BF16) for k in range(16)]
        attn = sb("attn", [128, T])
        x2c = sb("x2c", [128, T])
        psR = [ctx.enter_context(nc.psum_tensor(f"psR{i}", [128, 512], F32))
               for i in range(4)]
        psA, psB = psR[0], psR[1]

        p = Prog(nc, ctx)

        def f_const(e, fin):
            for _rep in range(2):
                fin(e.dma_start(out=carr[:], in_=carr_d), 16)
                fin(e.dma_start(out=nega[:], in_=alog), 16)
                fin(e.dma_start(out=dpt[:], in_=dpc), 16)
                fin(e.dma_start(out=onc[:], in_=onesc), 16)
                fin(e.dma_start(out=on1[:], in_=ones1), 16)
        s_const = p.step("sync", 160, f_const, name="const")

        def f_nega(e, fin):
            fin(e.activation(out=nega[:], in_=nega[:], func=AF.Exp), 1)
        s_negae = p.step("scalar", 1, f_nega, after=[s_const], name="negaexp")

        def f_negam(e, fin):
            fin(e.tensor_scalar_mul(nega[:], nega[:], -1.0), 1)
        s_nega = p.step("vector", 1, f_negam, after=[s_negae], fence=True,
                        plug=True, name="negamul")

        scan_steps, hdone_s = {}, {}
        ldc2_s, dxs_s, ldbt_s, bca_s, u_s, aa_s, ym_s, yfin_s = {}, {}, {}, {}, {}, {}, {}, {}
        cast_s = {}
        for c in range(16):
            cdep = [s_const]
            if c >= 1:
                cdep += [dxs_s[c - 1], aa_s[(c - 1, NST - 1)], yfin_s[c - 1]]

            def f_ldc2(e, fin, c=c):
                for _rep in range(2):
                    fin(e.dma_start(out=xcc[:], in_=xc_d[c * 128:(c + 1) * 128, :]), 16)
                    fin(e.dma_start(out=delta[:], in_=delta_d[c * 128:(c + 1) * 128, :]), 16)
            ldc2_s[c] = p.step("sync", 64, f_ldc2, after=cdep, name=f"ldc2_{c}")

            ddx = [ldc2_s[c]]
            if c >= 1:
                ddx.append(u_s[(c - 1, NST - 1, 3)])

            def f_dx(e, fin):
                fin(e.tensor_mul(dx[:], delta[:], xcc[:]), 1)
            dxs_s[c] = p.step("vector", 1, f_dx, after=ddx, plug=True,
                              name=f"dx{c}")

            def f_yz(e, fin):
                fin(e.memset(yacc[:], 0.0), 1)
            yz_dep = [yfin_s[c - 1]] if c >= 1 else []
            s_yz = p.step("vector", 1, f_yz, after=yz_dep, name=f"yz{c}")

            for n in range(NST):
                bdep = [s_const]
                if n >= 1:
                    bdep.append(ym_s[(c, n - 1, 3)])
                elif c >= 1:
                    bdep.append(ym_s[(c - 1, NST - 1, 3)])

                def f_ldbt(e, fin, n=n):
                    for _rep in range(2):
                        fin(e.dma_start(out=btrow[:], in_=bt_d[n:n + 1, :]), 16)
                ldbt_s[(c, n)] = p.step("scalar", 32, f_ldbt, after=bdep,
                                        name=f"ldbt{c}_{n}")

                for nt2, (no2, nw2) in enumerate(NT_T):
                    ps_n2 = psR[nt2]

                    def f_bca(e, fin, no2=no2, nw2=nw2, ps_n2=ps_n2):
                        fin(e.matmul(ps_n2[:, 0:nw2], on1[:],
                                     btrow[0:1, no2:no2 + nw2],
                                     start=True, stop=True), 1)
                    bdep2 = [ldbt_s[(c, n)]]
                    if n >= 1:
                        bdep2.append(ym_s[(c, n - 1, nt2)])
                    elif c >= 1:
                        bdep2.append(ym_s[(c - 1, NST - 1, nt2)])
                    bca_s[(c, n, nt2)] = p.step("tensor", 1, f_bca, after=bdep2,
                                                name=f"bca{c}_{n}_{nt2}")

                    def f_u(e, fin, no2=no2, nw2=nw2, ps_n2=ps_n2):
                        fin(e.tensor_mul(aa[:, no2:no2 + nw2],
                                         dx[:, no2:no2 + nw2], ps_n2[:, 0:nw2]), 1)
                    u_s[(c, n, nt2)] = p.step(
                        "vector", 1, f_u, plug=True,
                        after=[bca_s[(c, n, nt2)], dxs_s[c]] +
                              ([scan_steps[(c, n - 1)]] if (nt2 == 0 and n >= 1) else
                               ([scan_steps[(c - 1, NST - 1)]] if (nt2 == 0 and c >= 1) else [])),
                        name=f"u{c}_{n}_{nt2}")

                def f_aa(e, fin, c=c, n=n):
                    fin(e.activation(out=dsc[:], in_=delta[:], func=AF.Exp,
                                     scale=nega[:, c * NST + n:c * NST + n + 1]), 1)
                aa_s[(c, n)] = p.step("scalar", 1, f_aa,
                                      after=[ldc2_s[c], s_nega] +
                                            ([scan_steps[(c, n - 1)]] if n >= 1 else
                                             ([scan_steps[(c - 1, NST - 1)]] if c >= 1 else [])),
                                      plug=True, name=f"aa{c}_{n}")

                def f_scan(e, fin, c=c, n=n):
                    fin(e.tensor_tensor_scan(h[:], dsc[:], aa[:],
                                             carr[:, c * NST + n:c * NST + n + 1],
                                             op0=ALU.mult, op1=ALU.add), 1)
                scan_steps[(c, n)] = p.step(
                    "vector", 1, f_scan, plug=True,
                    after=[u_s[(c, n, 3)], aa_s[(c, n)]] +
                          ([ym_s[(c, n - 1, 3)]] if n >= 1 else
                           ([ym_s[(c - 1, NST - 1, 3)]] if c >= 1 else [])),
                    name=f"scan{c}_{n}")

                for nt2, (no2, nw2) in enumerate(NT_T):
                    ps_n2 = psR[nt2]

                    def f_ym(e, fin, no2=no2, nw2=nw2, ps_n2=ps_n2):
                        fin(e.tensor_mul(ytmp[:, 0:nw2], h[:, no2:no2 + nw2],
                                         ps_n2[:, 0:nw2]), 1)
                        fin(e.tensor_add(yacc[:, no2:no2 + nw2],
                                         yacc[:, no2:no2 + nw2], ytmp[:, 0:nw2]), 1)
                    ym_s[(c, n, nt2)] = p.step(
                        "vector", 2, f_ym, fence=True, plug=True,
                        after=[scan_steps[(c, n)], s_yz, bca_s[(c, n, nt2)]],
                        name=f"ym{c}_{n}_{nt2}")

            def f_ldz(e, fin, c=c):
                for _rep in range(2):
                    fin(e.dma_start(out=zt[:], in_=z_d[c * 128:(c + 1) * 128, :]), 16)
            zdep = [cast_s[c - 1]] if c >= 1 else []
            s_ldz = p.step("sync", 32, f_ldz, after=zdep, name=f"ldz{c}")

            def f_sz(e, fin):
                fin(e.activation(out=zt[:], in_=zt[:], func=AF.Silu), 1)
            s_sz = p.step("scalar", 1, f_sz, after=[s_ldz], plug=True,
                          name=f"sz{c}")

            def f_yfin(e, fin, c=c):
                fin(e.scalar_tensor_tensor(yacc[:], xcc[:], dpt[:, c:c + 1],
                                           yacc[:], op0=ALU.mult, op1=ALU.add), 1)
                fin(e.tensor_mul(ybf[c][:], yacc[:], zt[:]), 1)
            yfin_s[c] = p.step("vector", 2, f_yfin, fence=True, plug=True,
                               after=[ym_s[(c, NST - 1, 3)], s_sz, s_const],
                               name=f"yfin{c}")
            cast_s[c] = yfin_s[c]

        evw_s, odma_s, wmm_s = {}, {}, {}
        last_y = yfin_s[15]
        for m in range(8):
            wld_dep = [wmm_s[(m - 1, 3)]] if m >= 1 else []

            def f_wpml(e, fin, m=m):
                for _rep in range(2):
                    for k in range(16):
                        fin(e.dma_start(out=wpm[k][:],
                                        in_=wout_d[k * 128:(k + 1) * 128,
                                                   m * 128:(m + 1) * 128]), 16)
            s_wpml = p.step("gpsimd", 512, f_wpml, after=wld_dep,
                            name=f"wpml{m}")
            for nt, (no, nw) in enumerate(NT_T):
                ps_n = psA if nt % 2 == 0 else psB
                mdep = [last_y, s_wpml]
                if (m, nt - 2) in evw_s:
                    mdep.append(evw_s[(m, nt - 2)])
                elif m >= 1:
                    mdep.append(evw_s[(m - 1, nt + 2)])

                def f_wmm(e, fin, m=m, no=no, nw=nw, ps_n=ps_n):
                    for k in range(16):
                        fin(e.matmul(ps_n[:, 0:nw], wpm[k][:],
                                     ybf[k][:, no:no + nw],
                                     start=(k == 0), stop=(k == 15)), 1)
                s_wmm = p.step("tensor", 16, f_wmm, after=mdep, name=f"wmm{m}_{nt}")
                wmm_s[(m, nt)] = s_wmm

                evd = [s_wmm]
                if nt == 0:
                    evd.append(odma_s[m - 1] if m >= 1 else last_y)

                def f_evw(e, fin, no=no, nw=nw, ps_n=ps_n):
                    fin(e.copy(attn[:, no:no + nw], ps_n[:, 0:nw]), 1)
                evw_s[(m, nt)] = p.step("scalar", 1, f_evw, after=evd,
                                        name=f"evw{m}_{nt}")

            def f_ldx2(e, fin, m=m):
                fin(e.dma_start(out=x2c[:], in_=x2T[m * 128:(m + 1) * 128, :]), 16)
            s_ldx2 = p.step("sync", 16, f_ldx2,
                            after=[odma_s[m - 1]] if m >= 1 else [],
                            name=f"ldx2_{m}")

            def f_add(e, fin, m=m):
                fin(e.tensor_add(x2c[:], x2c[:], attn[:]), 1)
            s_add = p.step("vector", 1, f_add, plug=True,
                           after=[s_ldx2, evw_s[(m, 3)]], name=f"add{m}")

            def f_odma(e, fin, m=m):
                fin(e.dma_start(out=x3_d[m * 128:(m + 1) * 128, :], in_=x2c[:]), 16)
            odma_s[m] = p.step("sync", 16, f_odma, after=[s_add], name=f"odma{m}")

        p.step("sync", 0, lambda e, fin: None, after=[odma_s[7]], name="done")
        p.emit()
    return nc


def _emit_quant(e, fin, v, q_bf, P, W, scr):
    B = W // 16
    sm, inv, va, cnt, acc2 = scr["sm"], scr["inv"], scr["va"], scr["cnt"], scr["acc2"]
    v3 = v[0:P, 0:W].rearrange("p (b s) -> p b s", s=16)
    fin(e.tensor_reduce(sm[0:P, 0:B].rearrange("p (b o) -> p b o", o=1), v3,
                        axis=AX.X, op=ALU.max, apply_absolute_value=True), 1)
    fin(e.tensor_scalar(sm[0:P, 0:B], sm[0:P, 0:B], 1.0 / 6.0, 1e-12,
                        op0=ALU.mult, op1=ALU.max), 1)
    fin(e.reciprocal(inv[0:P, 0:B], sm[0:P, 0:B]), 1)
    fin(e.tensor_tensor(va[0:P, 0:W].rearrange("p (b s) -> p b s", s=16), v3,
                        inv[0:P, 0:B].rearrange("p (b o) -> p b o", o=1)
                        .broadcast_to((P, B, 16)), op=ALU.mult), 1)
    fin(e.tensor_scalar_mul(cnt[0:P, 0:W], va[0:P, 0:W], -1.0), 1)
    fin(e.tensor_max(va[0:P, 0:W], va[0:P, 0:W], cnt[0:P, 0:W]), 1)
    fin(e.tensor_single_scalar(cnt[0:P, 0:W], va[0:P, 0:W], 0.25, ALU.is_gt), 1)
    for thr in (0.75, 1.25, 1.75):
        fin(e.scalar_tensor_tensor(cnt[0:P, 0:W], va[0:P, 0:W], thr,
                                   cnt[0:P, 0:W], op0=ALU.is_gt, op1=ALU.add), 1)
    fin(e.tensor_single_scalar(acc2[0:P, 0:W], va[0:P, 0:W], 2.5, ALU.is_gt), 1)
    for thr in (3.5, 5.0, 5.0):
        fin(e.scalar_tensor_tensor(acc2[0:P, 0:W], va[0:P, 0:W], thr,
                                   acc2[0:P, 0:W], op0=ALU.is_gt, op1=ALU.add), 1)
    fin(e.scalar_tensor_tensor(cnt[0:P, 0:W], cnt[0:P, 0:W], 0.5,
                               acc2[0:P, 0:W], op0=ALU.mult, op1=ALU.add), 1)
    fin(e.tensor_single_scalar(va[0:P, 0:W], v[0:P, 0:W], 0.0, ALU.is_gt), 1)
    fin(e.scalar_tensor_tensor(acc2[0:P, 0:W], v[0:P, 0:W], 0.0,
                               va[0:P, 0:W], op0=ALU.is_lt, op1=ALU.subtract), 1)
    fin(e.tensor_mul(cnt[0:P, 0:W], cnt[0:P, 0:W], acc2[0:P, 0:W]), 1)
    fin(e.tensor_scalar_mul(cnt[0:P, 0:W], cnt[0:P, 0:W], -1.0), 1)
    fin(e.tensor_tensor(q_bf[0:P, 0:W].rearrange("p (b s) -> p b s", s=16),
                        cnt[0:P, 0:W].rearrange("p (b s) -> p b s", s=16),
                        sm[0:P, 0:B].rearrange("p (b o) -> p b o", o=1)
                        .broadcast_to((P, B, 16)), op=ALU.mult), 1)


N_QUANT_OPS = 20


def build_LH0():
    nc = bass.Bass("TRN2", target_bir_lowering=False)
    wg_sl = nc.dram_tensor("wg_sl", [352, D], F32, kind="ExternalInput").ap()
    wu_sl = nc.dram_tensor("wu_sl", [352, D], F32, kind="ExternalInput").ap()
    wd_sl = nc.dram_tensor("wd_sl", [128, 2816], F32, kind="ExternalInput").ap()
    wgq_o = nc.dram_tensor("wgq_o", [352, D], BF16, kind="ExternalOutput").ap()
    wuq_o = nc.dram_tensor("wuq_o", [352, D], BF16, kind="ExternalOutput").ap()
    wdq_o = nc.dram_tensor("wdq_o", [128, 2816], BF16, kind="ExternalOutput").ap()

    ctx = ExitStack()
    with ctx:
        sb = lambda name, shape, dt=F32: ctx.enter_context(nc.sbuf_tensor(name, shape, dt))
        v = sb("v", [128, 2816])
        qb = sb("qb", [128, 2816], BF16)
        scr = dict(sm=sb("sm", [128, 176]), inv=sb("inv", [128, 176]),
                   va=sb("va", [128, 2816]), cnt=sb("cnt", [128, 2816]),
                   acc2=sb("acc2", [128, 2816]))
        p = Prog(nc, ctx)

        jobs = []
        for src, dst in ((wg_sl, wgq_o), (wu_sl, wuq_o)):
            for r0, pr in ((0, 128), (128, 128), (256, 96)):
                jobs.append((src, dst, r0, pr, D))
        jobs.append((wd_sl, wdq_o, 0, 128, 2816))

        prev_q = None
        prev_st = None
        for ji, (src, dst, r0, pr, W) in enumerate(jobs):
            def f_ld(e, fin, src=src, r0=r0, pr=pr, W=W):
                for _rep in range(2):
                    fin(e.dma_start(out=v[0:pr, 0:W], in_=src[r0:r0 + pr, :]), 16)
            s_ld = p.step("sync", 32, f_ld,
                          after=[prev_q] if prev_q else [], name=f"ld{ji}")

            def f_q(e, fin, pr=pr, W=W):
                _emit_quant(e, fin, v, qb, pr, W, scr)
            deps = [s_ld]
            if prev_st is not None:
                deps.append(prev_st)
            prev_q = p.step("vector", N_QUANT_OPS, f_q, after=deps, fence=True,
                            plug=True, name=f"q{ji}")

            def f_st(e, fin, dst=dst, r0=r0, pr=pr, W=W):
                fin(e.dma_start(out=dst[r0:r0 + pr, :], in_=qb[0:pr, 0:W]), 16)
            prev_st = p.step("sync", 16, f_st, after=[prev_q], name=f"st{ji}")

        p.step("sync", 0, lambda e, fin: None, after=[prev_st], name="done")
        p.emit()
    return nc


def build_LH1():
    nc = bass.Bass("TRN2", target_bir_lowering=False)
    x3T = nc.dram_tensor("x3T", [D, T], F32, kind="ExternalInput").ap()
    wgqT_d = nc.dram_tensor("wgqT_d", [D, 2816], BF16, kind="ExternalInput").ap()
    wuqT_d = nc.dram_tensor("wuqT_d", [D, 2816], BF16, kind="ExternalInput").ap()
    wdqT_d = nc.dram_tensor("wdqT_d", [2816, D], BF16, kind="ExternalInput").ap()
    eye128 = nc.dram_tensor("eye128", [128, 128], F32, kind="ExternalInput").ap()
    onesc = nc.dram_tensor("onesc", [128, 1], F32, kind="ExternalInput").ap()
    ones1 = nc.dram_tensor("ones1", [1, 128], F32, kind="ExternalInput").ap()
    rowbounce = nc.dram_tensor("rowbounce", [2, T], F32, kind="Internal").ap()
    xq_dram = nc.dram_tensor("xq_dram", [T, D], BF16, kind="Internal").ap()
    hq_dram = nc.dram_tensor("hq_dram", [T, 2816], BF16, kind="Internal").ap()
    x3tm_d = nc.dram_tensor("x3tm_d", [T, D], F32, kind="Internal").ap()
    x4_d = nc.dram_tensor("x4_d", [T, D], F32, kind="ExternalOutput").ap()

    HH = 1408
    NT4 = ntiles(T, 512)
    NTH = ntiles(HH, 512)

    ctx = ExitStack()
    with ctx:
        sb = lambda name, shape, dt=F32: ctx.enter_context(nc.sbuf_tensor(name, shape, dt))
        wdc = [sb(f"wdc{j}", [128, D], BF16) for j in range(22)]
        wgc = [sb(f"wgc{k}", [128, HH], BF16) for k in range(8)]
        wuc = [sb(f"wuc{k}", [128, HH], BF16) for k in range(8)]
        xqfm = [sb(f"xqfm{k}", [128, T], BF16) for k in range(8)]
        g = sb("g", [128, HH])
        u = sb("u", [128, HH])
        va = sb("va", [128, HH])
        cnt = sb("cnt", [128, HH])
        acc2 = sb("acc2", [128, HH])
        scr = dict(sm=sb("sm", [128, 88]), inv=sb("inv", [128, 88]),
                   va=va, cnt=cnt, acc2=acc2)
        rowdon = sb("rowdon", [128, 2 * T])
        tmtN = sb("tmtN", [128, D])
        tmtR = sb("tmtR", [128, D])
        xqtm = sb("xqtm", [128, D], BF16)
        hq = sb("hq", [128, HH], BF16)
        hqt = [[sb(f"hqt{p_}_{j}", [128, 128], BF16) for j in range(22)]
               for p_ in range(2)]
        x3c = [sb(f"x3c{i}", [128, 128]) for i in range(2)]
        rcol = sb("rcol", [128, 16])
        ncol = sb("ncol", [128, 16])
        eye = sb("eye", [128, 128])
        onc = sb("onc", [128, 1])
        on1 = sb("on1", [1, 128])
        x3r = tmtR
        x4t = tmtN
        x0sub = [_SubView(va, 0), _SubView(cnt, 0)]
        sqsub = [_SubView(va, 512), _SubView(cnt, 512)]
        sumx, sumx2 = rowdon[0:1, 0:T], rowdon[32:33, 0:T]
        rtmp, rstd = rowdon[32:33, T:2 * T], rowdon[0:1, T:2 * T]
        nmr = sumx
        psT = ctx.enter_context(nc.psum_tensor("psT", [128, 128], F32))
        psG = ctx.enter_context(nc.psum_tensor("psG", [128, 512], F32))
        psU = ctx.enter_context(nc.psum_tensor("psU", [128, 512], F32))
        psD = ctx.enter_context(nc.psum_tensor("psD", [128, 1024], F32))

        p = Prog(nc, ctx)

        def f_const(e, fin):
            for _rep in range(2):
                fin(e.dma_start(out=eye[:], in_=eye128), 16)
                fin(e.dma_start(out=onc[:], in_=onesc), 16)
                fin(e.dma_start(out=on1[:], in_=ones1), 16)
        s_const = p.step("sync", 96, f_const, name="const")

        def f_wd(e, fin):
            for _rep in range(2):
                for j in range(22):
                    fin(e.dma_start(out=wdc[j][:],
                                    in_=wdqT_d[j * 128:(j + 1) * 128, :]), 16)
        s_wd = p.step("gpsimd", 704, f_wd, name="wd")

        ld_s, sq_s, mm_s, ev_s = {}, {}, {}, {}
        for ni, (no, nw) in enumerate(NT4):
            for k in range(8):
                par = k % 2
                it = (ni, k)
                prev = [s_const]
                key2 = (ni, k - 2) if k >= 2 else ((ni - 1, 6 + k) if ni else None)
                if key2 in mm_s:
                    prev.append(mm_s[key2])

                def f_ld(e, fin, k=k, no=no, nw=nw, par=par):
                    fin(e.dma_start(out=x0sub[par][:, 0:nw],
                                    in_=x3T[k * 128:(k + 1) * 128, no:no + nw]), 16)
                ld_s[it] = p.step("sync", 16, f_ld, after=prev, name=f"ld{it}")

                def f_sq(e, fin, nw=nw, par=par):
                    fin(e.activation(out=sqsub[par][:, 0:nw],
                                     in_=x0sub[par][:, 0:nw], func=AF.Square))
                sq_s[it] = p.step("scalar", 1, f_sq, after=[ld_s[it]] + prev,
                                  name=f"sq{it}")

                def f_mm(e, fin, k=k, nw=nw, par=par):
                    fin(e.matmul(psG[0:1, 0:nw], onc[:], x0sub[par][:, 0:nw],
                                 start=(k == 0), stop=(k == 7)))
                    fin(e.matmul(psU[0:1, 0:nw], onc[:], sqsub[par][:, 0:nw],
                                 start=(k == 0), stop=(k == 7)))
                mm_s[it] = p.step(
                    "tensor", 2, f_mm,
                    after=[ld_s[it], sq_s[it]] +
                          ([ev_s[ni - 1]] if (k == 0 and ni > 0) else []),
                    name=f"mm{it}")

            def f_ev(e, fin, no=no, nw=nw):
                fin(e.copy(rowdon[0:1, no:no + nw], psG[0:1, 0:nw]))
                fin(e.copy(rowdon[32:33, no:no + nw], psU[0:1, 0:nw]))
            ev_s[ni] = p.step("scalar", 2, f_ev, after=[mm_s[(ni, 7)]],
                              name=f"ev{ni}")

        def f_rm(e, fin):
            fin(e.tensor_scalar_mul(sumx, sumx, 1.0 / D))
            fin(e.tensor_scalar_mul(sumx2, sumx2, 1.0 / D))
            fin(e.tensor_mul(rtmp, sumx, sumx))
            fin(e.tensor_sub(sumx2, sumx2, rtmp))
            fin(e.tensor_scalar_add(sumx2, sumx2, LN_EPS))
        s_rm = p.step("vector", 5, f_rm, after=[ev_s[3]], name="rm", fence=True)

        def f_rs(e, fin):
            fin(e.activation(out=rtmp, in_=sumx2, func=AF.Ln))
            fin(e.activation(out=rstd, in_=rtmp, func=AF.Exp, scale=-0.5))
        s_rs = p.step("scalar", 2, f_rs, after=[s_rm], name="rs", fence=True)

        def f_nm(e, fin):
            fin(e.tensor_mul(nmr, sumx, rstd))
            fin(e.tensor_scalar_mul(nmr, nmr, -1.0))
        s_nm = p.step("vector", 2, f_nm, after=[s_rs], name="nm", fence=True)

        def f_rowout(e, fin):
            fin(e.dma_start(out=rowbounce[0:1, :], in_=rstd), 16)
            fin(e.dma_start(out=rowbounce[1:2, :], in_=nmr), 16)
        s_rowout = p.step("sync", 32, f_rowout, after=[s_nm], name="rowout")

        def f_rowin(e, fin):
            for _rep in range(2):
                fin(e.dma_start(out=rcol[:],
                                in_=rowbounce[0:1, :].rearrange("o (a b) -> (o b) a", b=128),
                                allow_slow_non_contiguous=True), 16)
                fin(e.dma_start(out=ncol[:],
                                in_=rowbounce[1:2, :].rearrange("o (a b) -> (o b) a", b=128),
                                allow_slow_non_contiguous=True), 16)
        s_rowin = p.step("sync", 64, f_rowin, after=[s_rowout], name="rowin")

        ldt_s, tp_s, evR_s, evN_s, q2_s, x3st_s, xqst_s = {}, {}, {}, {}, {}, {}, {}
        for tc in range(16):
            for m in range(8):
                par = m % 2
                dep = []
                if (tc, m - 1) in evR_s:
                    pass
                if tc == 0 and m == 0:
                    dep.append(ev_s[3])

                def f_ldt(e, fin, m=m, tc=tc, par=par):
                    fin(e.dma_start(out=x3c[par][:],
                                    in_=x3T[m * 128:(m + 1) * 128,
                                            tc * 128:(tc + 1) * 128]), 16)
                ldt_s[(tc, m)] = p.step("scalar", 16, f_ldt, after=dep,
                                        name=f"ldt{tc}_{m}")

                tdep = [ldt_s[(tc, m)], s_const]
                prevkey = (tc, m - 1) if m else ((tc - 1, 7) if tc else None)
                if prevkey in evN_s:
                    tdep += [evN_s[prevkey], evR_s[prevkey]]

                def f_tp(e, fin, par=par):
                    fin(e.transpose(psT[:], x3c[par][:], eye[:]))
                tp_s[(tc, m)] = p.step("tensor", 1, f_tp, after=tdep,
                                       name=f"tp{tc}_{m}")

                rdep = [tp_s[(tc, m)]]
                if tc >= 1 and m == 0:
                    rdep.append(x3st_s[tc - 1])

                def f_evR(e, fin, m=m):
                    fin(e.copy(tmtR[:, m * 128:(m + 1) * 128], psT[:]))
                evR_s[(tc, m)] = p.step("scalar", 1, f_evR, after=rdep,
                                        plug=True, name=f"evR{tc}_{m}")

                ndep = [tp_s[(tc, m)], s_rowin]
                if tc >= 1 and m == 0:
                    ndep.append(q2_s[tc - 1])

                def f_evN(e, fin, m=m, tc=tc):
                    fin(e.tensor_scalar(tmtN[:, m * 128:(m + 1) * 128],
                                        psT[:], rcol[:, tc:tc + 1],
                                        ncol[:, tc:tc + 1],
                                        op0=ALU.mult, op1=ALU.add))
                evN_s[(tc, m)] = p.step("vector", 1, f_evN, plug=True,
                                        after=ndep, name=f"evN{tc}_{m}")

            def f_x3st(e, fin, tc=tc):
                for _rep in range(2):
                    fin(e.dma_start(out=x3tm_d[tc * 128:(tc + 1) * 128, :],
                                    in_=tmtR[:]), 16)
            x3st_s[tc] = p.step("sync", 32, f_x3st, after=[evR_s[(tc, 7)]],
                                name=f"x3st{tc}")

            qdep = [evN_s[(tc, 7)]]
            if tc >= 1:
                qdep.append(xqst_s[tc - 1])

            def f_q2(e, fin):
                _emit_quant(e, fin, tmtN, xqtm, 128, D, scr)
            q2_s[tc] = p.step("vector", N_QUANT_OPS, f_q2, after=qdep,
                              fence=True, plug=True, name=f"q2_{tc}")

            def f_xqst(e, fin, tc=tc):
                for _rep in range(2):
                    fin(e.dma_start(out=xq_dram[tc * 128:(tc + 1) * 128, :],
                                    in_=xqtm[:]), 16)
            xqst_s[tc] = p.step("sync", 32, f_xqst, after=[q2_s[tc]],
                                name=f"xqst{tc}")

        def f_xqf(e, fin):
            for _rep in range(2):
                for k in range(8):
                    fin(e.dma_start_transpose(xqfm[k][:],
                                              xq_dram[:, k * 128:(k + 1) * 128]), 16)
        s_xqf = p.step("scalar", 256, f_xqf, after=[xqst_s[15]], name="xqf")

        wl_s, gev_s, hmul_s, qh_s, hqst_s, gmm_s = {}, {}, {}, {}, {}, {}
        for hf in range(2):
            wdep = [qh_s[(0, 15)]] if hf == 1 else []

            def f_wl(e, fin, hf=hf):
                for _rep in range(2):
                    for k in range(8):
                        fin(e.dma_start(out=wgc[k][:],
                                        in_=wgqT_d[k * 128:(k + 1) * 128,
                                                   hf * HH:(hf + 1) * HH]), 16)
                        fin(e.dma_start(out=wuc[k][:],
                                        in_=wuqT_d[k * 128:(k + 1) * 128,
                                                   hf * HH:(hf + 1) * HH]), 16)
            wl_s[hf] = p.step("gpsimd", 512, f_wl, after=wdep, name=f"wl{hf}")

            for tc in range(16):
                for nt, (no, nw) in enumerate(NTH):
                    mdep = [wl_s[hf], s_xqf]
                    prevk = (hf, tc, nt - 1) if nt else \
                            ((hf, tc - 1, 2) if tc else ((0, 15, 2) if hf else None))
                    if prevk in gev_s:
                        mdep.append(gev_s[prevk])
                    if nt == 0 and (hf, tc - 1) in qh_s:
                        mdep.append(qh_s[(hf, tc - 1)])
                    elif nt == 0 and hf == 1 and tc == 0:
                        mdep.append(qh_s[(0, 15)])
                    if hf == 0 and tc == 0 and nt == 0:
                        mdep.append(ev_s[3])

                    def f_gmm(e, fin, tc=tc, no=no, nw=nw):
                        for k in range(8):
                            fin(e.matmul(psG[:, 0:nw],
                                         xqfm[k][:, tc * 128:(tc + 1) * 128],
                                         wgc[k][:, no:no + nw],
                                         start=(k == 0), stop=(k == 7)))
                        for k in range(8):
                            fin(e.matmul(psU[:, 0:nw],
                                         xqfm[k][:, tc * 128:(tc + 1) * 128],
                                         wuc[k][:, no:no + nw],
                                         start=(k == 0), stop=(k == 7)))
                    gmm_s[(hf, tc, nt)] = p.step("tensor", 16, f_gmm, after=mdep,
                                                 name=f"gmm{hf}_{tc}_{nt}")

                    def f_gev(e, fin, no=no, nw=nw):
                        fin(e.copy(g[:, no:no + nw], psG[:, 0:nw]))
                        fin(e.copy(u[:, no:no + nw], psU[:, 0:nw]))
                    gev_s[(hf, tc, nt)] = p.step("scalar", 2, f_gev,
                                                 after=[gmm_s[(hf, tc, nt)]],
                                                 plug=True, name=f"gev{hf}_{tc}_{nt}")

                def f_silu(e, fin):
                    fin(e.activation(out=cnt[:, 0:HH], in_=g[:, 0:HH], func=AF.Silu))
                s_silu = p.step("scalar", 1, f_silu, after=[gev_s[(hf, tc, 2)]],
                                name=f"silu{hf}_{tc}")

                def f_hmul(e, fin):
                    fin(e.tensor_tensor(g[:, 0:HH], cnt[:, 0:HH], u[:, 0:HH],
                                        op=ALU.mult))
                hmul_s[(hf, tc)] = p.step("vector", 1, f_hmul, after=[s_silu],
                                          plug=True, name=f"hmul{hf}_{tc}")

                hdep = [hmul_s[(hf, tc)]]
                if (hf, tc - 1) in hqst_s:
                    hdep.append(hqst_s[(hf, tc - 1)])
                elif hf == 1 and tc == 0:
                    hdep.append(hqst_s[(0, 15)])

                def f_qh(e, fin):
                    _emit_quant(e, fin, g, hq, 128, HH, scr)
                qh_s[(hf, tc)] = p.step("vector", N_QUANT_OPS, f_qh,
                                        after=hdep, fence=True, plug=True,
                                        name=f"qh{hf}_{tc}")

                def f_hqst(e, fin, hf=hf, tc=tc):
                    for _rep in range(2):
                        fin(e.dma_start(out=hq_dram[tc * 128:(tc + 1) * 128,
                                                    hf * HH:(hf + 1) * HH],
                                        in_=hq[:]), 16)
                hqst_s[(hf, tc)] = p.step("sync", 32, f_hqst,
                                          after=[qh_s[(hf, tc)]],
                                          name=f"hqst{hf}_{tc}")

        tl5_s, x3r_s, mm5_s, res_s, x4st_s = {}, {}, {}, {}, {}
        for tc in range(16):
            par = tc % 2
            tdep = [hqst_s[(1, 15)]]
            if tc >= 2:
                tdep.append(res_s[tc - 2])

            def f_tl5(e, fin, tc=tc, par=par):
                for _rep in range(2):
                    for j in range(22):
                        fin(e.dma_start_transpose(
                            hqt[par][j][:],
                            hq_dram[tc * 128:(tc + 1) * 128,
                                    j * 128:(j + 1) * 128]), 16)
            tl5_s[tc] = p.step("scalar", 704, f_tl5, after=tdep, name=f"tl5_{tc}")

            rdep = [x3st_s[tc]]
            if tc >= 1:
                rdep.append(res_s[tc - 1])

            def f_x3r(e, fin, tc=tc):
                for _rep in range(2):
                    fin(e.dma_start(out=x3r[:], in_=x3tm_d[tc * 128:(tc + 1) * 128, :]), 16)
            x3r_s[tc] = p.step("sync", 32, f_x3r, after=rdep, name=f"x3r{tc}")

            mdep = [tl5_s[tc], s_wd]
            if tc >= 1:
                mdep.append(res_s[tc - 1])

            def f_mm5(e, fin, par=par):
                for n0 in (0, 512):
                    for j in range(22):
                        fin(e.matmul(psD[:, n0:n0 + 512], hqt[par][j][:],
                                     wdc[j][:, n0:n0 + 512],
                                     start=(j == 0), stop=(j == 21)))
            mm5_s[tc] = p.step("tensor", 44, f_mm5, after=mdep, name=f"mm5_{tc}")

            sdep = [mm5_s[tc], x3r_s[tc]]
            if tc >= 1:
                sdep.append(x4st_s[tc - 1])

            def f_res(e, fin):
                fin(e.tensor_tensor(x4t[:, 0:512], psD[:, 0:512],
                                    x3r[:, 0:512], op=ALU.add))
                fin(e.tensor_tensor(x4t[:, 512:1024], psD[:, 512:1024],
                                    x3r[:, 512:1024], op=ALU.add))
            res_s[tc] = p.step("vector", 2, f_res, after=sdep, plug=True,
                               name=f"res{tc}")

            def f_x4st(e, fin, tc=tc):
                fin(e.dma_start(out=x4_d[tc * 128:(tc + 1) * 128, :], in_=x4t[:]), 16)
            x4st_s[tc] = p.step("sync", 16, f_x4st, after=[res_s[tc]],
                                name=f"x4st{tc}")

        p.step("sync", 0, lambda e, fin: None, after=[x4st_s[15]], name="done")
        p.emit()
    return nc


ONESC = np.ones((128, 1), np.float32)
ONES1 = np.ones((1, 128), np.float32)
EYE128 = np.eye(128, dtype=np.float32)
IOTA8C = np.repeat(np.arange(8, dtype=np.float32)[None, :], 128, axis=0)
IOTA8R = np.arange(8, dtype=np.float32).reshape(8, 1)
ONES8 = np.ones((8, 1), np.float32)


def _shard_ranges(c):
    b, h = c // 2, c % 2
    return b, h * T, (h + 1) * T


def _stage_x0e(x, c):
    b, s0, s1 = _shard_ranges(c)
    halo = np.zeros((3, D), np.float32) if s0 == 0 else x[b, s0 - 3:s0]
    xe = np.concatenate([halo, x[b, s0:s1]], axis=0)
    return np.ascontiguousarray(xe.T.astype(np.float32))


def _stage_wxT(W_x):
    w = W_x[16:32].T.astype(np.float32)
    w = w.reshape(16, 128, 16)
    return np.ascontiguousarray(w.transpose(1, 0, 2).reshape(128, 256))


def _stage_convw(conv_w):
    w = conv_w.reshape(DI, 4).astype(np.float32).reshape(16, 128, 4)
    return np.ascontiguousarray(w.transpose(1, 0, 2).reshape(128, 64))


def _stage_convb(conv_b):
    return np.ascontiguousarray(conv_b.astype(np.float32).reshape(16, 128).T)


def _chunkmajor16(w):
    return np.ascontiguousarray(
        w.reshape(16, 128, 16).transpose(1, 0, 2).reshape(128, 256))


_progs = {}


def _prog(name, builder):
    if name not in _progs:
        _progs[name] = builder()
    return _progs[name]


def kernel(**inp):
    f32 = lambda k: np.asarray(inp[k]).astype(np.float32)
    x = f32("L0_attn_x")
    cores = list(range(8))

    w_gate, w_up, w_down = f32("L1_ffn_w_gate"), f32("L1_ffn_w_up"), f32("L1_ffn_w_down")
    maps = [dict(wg_sl=np.ascontiguousarray(w_gate[c * 352:(c + 1) * 352]),
                 wu_sl=np.ascontiguousarray(w_up[c * 352:(c + 1) * 352]),
                 wd_sl=np.ascontiguousarray(w_down[c * 128:(c + 1) * 128]))
            for c in cores]
    resH0 = run_bass_kernel_spmd(_prog("LH0", build_LH0), maps, core_ids=cores)
    wgqT = np.ascontiguousarray(
        np.vstack([np.asarray(resH0.results[c]["wgq_o"]) for c in cores]).T)
    wuqT = np.ascontiguousarray(
        np.vstack([np.asarray(resH0.results[c]["wuq_o"]) for c in cores]).T)
    wdqT = np.ascontiguousarray(
        np.vstack([np.asarray(resH0.results[c]["wdq_o"]) for c in cores]).T)

    winT_A = np.ascontiguousarray(f32("L0_attn_W_in").T)
    maps = [dict(x0e=_stage_x0e(x, c), winT=winT_A,
                 wxT=_stage_wxT(f32("L0_attn_W_x")),
                 convw=_stage_convw(f32("L0_attn_conv_w")),
                 convb=_stage_convb(f32("L0_attn_conv_b")),
                 onesc=ONESC, ones1=ONES1) for c in cores]
    resA = run_bass_kernel_spmd(_prog("LA", build_LA), maps, core_ids=cores)

    blogT = np.ascontiguousarray(f32("L0_attn_B_log").T)
    woutT = np.ascontiguousarray(f32("L0_attn_W_out").T)
    lng = np.ascontiguousarray(f32("L0_attn_ln_g").reshape(8, 128).T)
    lnb = np.ascontiguousarray(f32("L0_attn_ln_b").reshape(8, 128).T)
    maps = []
    for c in cores:
        b, s0, s1 = _shard_ranges(c)
        carry = (np.zeros((128, 16), np.float32) if c % 2 == 0
                 else np.asarray(resA.results[c - 1]["hend_d"]))
        maps.append(dict(
            xcs_d=np.asarray(resA.results[c]["xcs_d"]),
            z_d=np.asarray(resA.results[c]["z_d"]),
            sct_d=np.asarray(resA.results[c]["sct_d"]),
            carry_d=carry,
            x0T=np.ascontiguousarray(x[b, s0:s1].T),
            blogT=blogT, woutT=woutT, lng=lng, lnb=lnb,
            onesc=ONESC, ones1=ONES1))
    resB = run_bass_kernel_spmd(_prog("LB", build_LB), maps, core_ids=cores)
    x1T = [np.asarray(resB.results[c]["x1_d"]) for c in cores]

    wr = np.ascontiguousarray(f32("L0_ffn_w_router"))
    maps = [dict(x1T=x1T[c], wr=wr, iota8c=IOTA8C,
                 eye128=EYE128, onesc=ONESC, ones1=ONES1) for c in cores]
    resC = run_bass_kernel_spmd(_prog("LC", build_LC), maps, core_ids=cores)

    e1 = np.concatenate([np.asarray(resC.results[c]["e1_d"]).T.reshape(T) for c in cores])
    e2 = np.concatenate([np.asarray(resC.results[c]["e2_d"]).T.reshape(T) for c in cores])
    eflat = np.stack([e1, e2], axis=1).reshape(1, 4 * 4096 * 2).astype(np.float32)
    xn1f = np.ascontiguousarray(
        np.concatenate([np.asarray(resC.results[c]["xn1_d"]) for c in cores], axis=0))

    maps = []
    for c in cores:
        tokm = ((np.arange(4096) + c * 4096) // 2).astype(np.uint32)
        tokmap = np.repeat(tokm.reshape(32, 128).T, 2, axis=1).copy()
        maps.append(dict(eflat=eflat, iota8r=IOTA8R, ones8=ONES8,
                         tokmap=tokmap, mycid=np.array([[c]], dtype=np.uint32)))
    resD1 = run_bass_kernel_spmd(_prog("LD1", build_LD1), maps, core_ids=cores)

    w1, w2 = f32("L0_ffn_w1"), f32("L0_ffn_w2")
    parts_sliced = []
    for c in cores:
        stack = []
        for pt in cores:
            part = np.asarray(resD1.results[pt]["s2tp_d"]).reshape(128, 256, 2)
            stack.append(part[:, c * 32:(c + 1) * 32, :].reshape(128, 64))
        parts_sliced.append(np.stack(stack, axis=0))
    maps = [dict(parts=parts_sliced[c], xn1f=xn1f,
                 w1_d=np.ascontiguousarray(w1[c]),
                 w2_d=np.ascontiguousarray(w2[c])) for c in cores]
    resD2 = run_bass_kernel_spmd(_prog("LD2", build_LD2), maps, core_ids=cores)

    bias = f32("L0_ffn_bias").reshape(D)
    yall = np.ascontiguousarray(np.concatenate(
        [np.asarray(resD2.results[c]["y_d"]) for c in cores]
        + [np.zeros((1, D), np.float32)], axis=0))
    sv0 = np.asarray(resD1.results[0]["slotv_d"])
    sv_flat = np.ascontiguousarray(sv0.T).reshape(8 * 4096)
    biasr = np.ascontiguousarray(np.repeat(bias[None, :], 128, 0))
    maps = []
    for c in cores:
        my = sv_flat[c * 4096:(c + 1) * 4096]
        maps.append(dict(
            x1T=x1T[c], yall=yall,
            slot1=np.ascontiguousarray(my[0::2].astype(np.uint32).reshape(16, 128).T),
            slot2=np.ascontiguousarray(my[1::2].astype(np.uint32).reshape(16, 128).T),
            g1c=np.asarray(resC.results[c]["g1_d"]),
            g2c=np.asarray(resC.results[c]["g2_d"]),
            biasr=biasr, eye128=EYE128))
    resE = run_bass_kernel_spmd(_prog("LE", build_LE), maps, core_ids=cores)
    x2T = [np.asarray(resE.results[c]["x2_d"]) for c in cores]

    winT_F = np.ascontiguousarray(f32("L1_attn_W_in"))
    wxpT = _chunkmajor16(np.ascontiguousarray(f32("L1_attn_W_xproj")[:, 16:32]))
    wdt_d = np.ascontiguousarray(f32("L1_attn_W_dt"))
    bdt = np.ascontiguousarray(f32("L1_attn_b_dt").reshape(16, 128).T)
    alog = _chunkmajor16(f32("L1_attn_A_log"))
    maps = []
    for c in cores:
        halo = (np.zeros((1024, 3), np.float32) if c % 2 == 0
                else x2T[c - 1][:, T - 3:T])
        x2e = np.ascontiguousarray(np.concatenate([halo, x2T[c]], axis=1))
        maps.append(dict(
            x2e=x2e, winT=winT_F, wxpT=wxpT, wdt_d=wdt_d, bdt=bdt, alog=alog,
            convw=_stage_convw(f32("L1_attn_conv_w")),
            convb=_stage_convb(f32("L1_attn_conv_b")),
            onesc=ONESC, ones1=ONES1))
    resF = run_bass_kernel_spmd(_prog("LF", build_LF), maps, core_ids=cores)

    dpc = np.ascontiguousarray(f32("L1_attn_Dp").reshape(16, 128).T)
    wout_d = np.ascontiguousarray(f32("L1_attn_W_out"))
    maps = []
    for c in cores:
        rF = resF.results[c]
        carr = (np.zeros((128, 256), np.float32) if c % 2 == 0
                else np.asarray(resF.results[c - 1]["hend_d"]))
        maps.append(dict(
            xc_d=np.asarray(rF["xc_d"]), z_d=np.asarray(rF["z_d"]),
            delta_d=np.asarray(rF["delta_d"]), bt_d=np.asarray(rF["bt_d"]),
            carr_d=carr, alog=alog, dpc=dpc, x2T=x2T[c], wout_d=wout_d,
            onesc=ONESC, ones1=ONES1))
    resG = run_bass_kernel_spmd(_prog("LG", build_LG), maps, core_ids=cores)

    maps = [dict(x3T=np.asarray(resG.results[c]["x3_d"]),
                 wgqT_d=wgqT, wuqT_d=wuqT, wdqT_d=wdqT,
                 eye128=EYE128, onesc=ONESC, ones1=ONES1) for c in cores]
    resH1 = run_bass_kernel_spmd(_prog("LH1", build_LH1), maps, core_ids=cores)

    out = np.empty((4, 4096, D), np.float32)
    for c in cores:
        b, s0, s1 = _shard_ranges(c)
        out[b, s0:s1, :] = np.asarray(resH1.results[c]["x4_d"])
    return out
